# version 16
# baseline (speedup 1.0000x reference)
import contextlib
import numpy as np
import concourse.bass as bass
import concourse.mybir as mybir
from concourse.bass_utils import run_bass_kernel_spmd

F32 = mybir.dt.float32
F32R = mybir.dt.float32r
BF16 = mybir.dt.bfloat16
I32 = mybir.dt.int32
IOA = bass.IndirectOffsetOnAxis
AF = mybir.ActivationFunctionType
OP = mybir.AluOpType
AX = mybir.AxisListType

T = 2048
D = 1024
DIN = 2048
NST = 16
DTR = 64
NE = 32
NQ = 4
TQ = T // NQ
NH = 2
TH = T // NH
NTT = T // 128
KC = D // 128
CAP = 192
BIGC = 16384.0

ENGS = ("sync", "gpsimd", "vector", "scalar", "tensor")
SEMS = ("sync", "gpsimd", "vector", "scalar", "tensor", "syncdl", "gpdl", "scdl", "vedl")
DLSEM = {"sync": "syncdl", "gpsimd": "gpdl", "scalar": "scdl", "vector": "vedl"}


class Plan:

    def __init__(self):
        self.ops = {e: [] for e in ENGS}
        self.cnt = {e: 0 for e in SEMS}
        self.lastwait = {e: {} for e in ENGS}
        self.w = {}
        self.r = {}
        self.pending_fence = {}
        self.pending_dma = {}

    def fence(self):
        toks = [(e, self.cnt[e]) for e in SEMS if self.cnt[e] > 0]
        for e in ENGS:
            self.pending_fence[e] = list(toks)

    def _op(self, eng, sem, fn, waits, inc):
        if self.pending_fence.get(eng):
            waits = list(waits) + self.pending_fence.pop(eng)
        ws = []
        for tok in waits:
            if tok is None:
                continue
            pe, val = tok
            if pe == eng:
                continue
            if self.lastwait[eng].get(pe, -1) >= val:
                continue
            self.lastwait[eng][pe] = val
            ws.append((pe, val))
        self.ops[eng].append((ws, fn, inc, sem))
        self.cnt[sem] += inc
        return (sem, self.cnt[sem])

    def do(self, eng, fn, reads=(), writes=(), extra=(), dma=False, defer=False):
        waits = list(extra)
        for n in reads:
            waits.append(self.w.get(n))
        for n in writes:
            waits.append(self.w.get(n))
            waits.extend(self.r.get(n, ()))
        if dma:
            dsem = DLSEM[eng]
            self._op(eng, dsem, fn, waits, 16)
            if defer:
                pd = self.pending_dma.setdefault(eng, [])
                pd.extend(("w", n) for n in writes)
                pd.extend(("r", n) for n in reads)
                return None
            pend = self.pending_dma.pop(eng, [])
            tok = self._op(eng, eng, None, [(dsem, self.cnt[dsem])], 1)
            for kind, n in pend:
                if kind == "w":
                    self.w[n] = tok
                    self.r[n] = []
                else:
                    self.r.setdefault(n, []).append(tok)
        else:
            tok = self._op(eng, eng, fn, waits, 1)
        for n in reads:
            self.r.setdefault(n, []).append(tok)
        for n in writes:
            self.w[n] = tok
            self.r[n] = []
        return tok

    def dma_flush(self, eng):
        pend = self.pending_dma.pop(eng, None)
        if not pend:
            return None
        dsem = DLSEM[eng]
        tok = self._op(eng, eng, None, [(dsem, self.cnt[dsem])], 1)
        for kind, n in pend:
            if kind == "w":
                self.w[n] = tok
                self.r[n] = []
            else:
                self.r.setdefault(n, []).append(tok)
        return tok

    def token(self, sem):
        return (sem, self.cnt[sem])


def build_program(debug=False, stage=99):
    nc = bass.Bass("TRN2", target_bir_lowering=False, detect_race_conditions=False)

    dx = nc.dram_tensor("x", [T, D], F32, kind="ExternalInput").ap()
    d_win = nc.dram_tensor("w_in", [D, DIN], F32, kind="ExternalInput").ap()
    d_kx = nc.dram_tensor("kx", [128, 32], F32, kind="ExternalInput").ap()
    d_kz = nc.dram_tensor("kz", [128, 32], F32, kind="ExternalInput").ap()
    d_wxp = nc.dram_tensor("w_xproj", [D, 96], F32, kind="ExternalInput").ap()
    d_wdt = nc.dram_tensor("w_dt", [DTR, D], F32, kind="ExternalInput").ap()
    d_bdt = nc.dram_tensor("b_dt", [128, KC], F32, kind="ExternalInput").ap()
    d_alog = nc.dram_tensor("a_log", [128, 128], F32, kind="ExternalInput").ap()
    d_dd = nc.dram_tensor("d_d", [128, KC], F32, kind="ExternalInput").ap()
    d_wout = nc.dram_tensor("w_out", [DIN, D], F32, kind="ExternalInput").ap()
    d_bout = nc.dram_tensor("b_out", [1, D], F32, kind="ExternalInput").ap()
    d_rw = nc.dram_tensor("r_wT", [D, NE], F32, kind="ExternalInput").ap()
    d_we = nc.dram_tensor("w_e", [NE, D, D], F32, kind="ExternalInput").ap()
    d_c = nc.dram_tensor("c_row", [128, KC], F32, kind="ExternalInput").ap()
    d_wmod = nc.dram_tensor("wmod", [D, 6 * D], F32, kind="ExternalInput").ap()
    d_out = nc.dram_tensor("out", [T, D], F32, kind="ExternalOutput").ap()
    d_x2s = nc.dram_tensor("x2s", [T, D], F32, kind="Internal").ap()
    d_h2s = nc.dram_tensor("h2s", [T, D], F32R, kind="Internal").ap()
    d_tok = nc.dram_tensor("dtok", [NH * 6400, 1], I32, kind="Internal").ap()
    d_ctr = nc.dram_tensor("dctr", [NH * 6400, D], F32, kind="Internal").ap()
    d_werb = nc.dram_tensor("werb", [NE, D, D], BF16, kind="Internal").ap()
    d_winr = nc.dram_tensor("winr", [D, DIN], F32R, kind="Internal").ap()
    d_woutr = nc.dram_tensor("woutr", [DIN, D], F32R, kind="Internal").ap()
    eye = nc.inline_tensor(np.eye(128, dtype=np.float32), "eye").ap()
    toki_c = nc.inline_tensor(
        np.ascontiguousarray(np.arange(T, dtype=np.int32).reshape(NTT, 128).T
                             ).view(np.float32), "toki_c").ap()
    erow_c = nc.inline_tensor(
        (np.arange(NE, dtype=np.float32) * CAP
         + np.array([[0.0], [6400.0]], np.float32)), "erow_c").ap()

    dbg = {}
    if debug:
        for nm, shp in (("xh", [KC, 128, T]), ("z", [KC, 128, T]),
                        ("delta", [KC, 128, T]), ("y", [KC, 128, T]),
                        ("wm", [128, NTT, NE]), ("h2", [KC, 128, T]),
                        ("x2", [T, D]), ("hfm", [KC, 128, 768]),
                        ("mod", [128, 6 * D])):
            dbg[nm] = nc.dram_tensor("dbg_" + nm, shp, F32, kind="ExternalOutput").ap()

    p = Plan()

    def ga(name, shape, dt):
        return nc.alloc_sbuf_tensor(name, shape, dt)

    ones = ga("ones", [128, 128], F32R)
    identr = ga("identr", [128, 128], F32R)
    identf = ga("identf", [128, 128], F32)
    toki = ga("toki", [128, NTT], I32)
    erows = ga("erows", [128, 2, NE], F32)
    modB = ga("modB", [128, 3 * D], F32)
    rwt = ga("rwt", [128, KC, NE], F32R)
    cols = ga("cols", [128, 256], F32)
    stl = ga("stl", [128, 256], F32)

    def settle(eng, resources):
        if eng == "vector":
            p.do(eng, lambda e: e.tensor_copy(out=stl[:, 128:256], in_=stl[:, 0:128]),
                 writes=list(resources))
        else:
            p.do(eng, lambda e: e.copy(stl[:, 128:256], stl[:, 0:128]),
                 writes=list(resources))
    xtile = ga("xtile", [128, 2, D], F32)
    escr = ga("escr", [128, 2, D], F32)
    htm2 = ga("htm2", [128, 2, D], F32R)
    sqscr = ga("sqscr", [128, D], F32)
    recipc = ga("recipc", [128, NTT], F32)
    wmsk = ga("wmsk", [128, NTT, NE], F32)

    stE = contextlib.ExitStack()

    def ea(name, shape, dt):
        return stE.enter_context(nc.sbuf_tensor(name, shape, dt))

    modA = ea("modA", [128, 3 * D], F32)
    gab = ea("gab", [128, D], F32)
    kxc = ea("kxc", [128, 32], F32)
    kzc = ea("kzc", [128, 32], F32)
    aneg = ea("aneg", [128, 128], F32)
    bdt2 = ea("bdt2", [128, KC], F32)
    ddc = ea("ddc", [128, KC], F32)
    wxp = ea("wxp", [128, KC, 96], F32R)
    wdt = ea("wdt", [DTR, D], F32R)
    hstash = ea("hstash", [128, 512], F32)

    pAB = [nc.alloc_psum_tensor(nm, [128, 1024], F32) for nm in ("pA", "pB", "pC", "pD")]
    pA, pB, pC, pD = pAB
    tr0 = pD[:, 0:128].bitcast(F32R)
    tr1 = pD[:, 512:640].bitcast(F32R)

    def bankres(i):
        return ("ps%d_0" % i, "ps%d_1" % i)

    st0 = contextlib.ExitStack()
    silu_c = st0.enter_context(nc.sbuf_tensor("silu_c", [128, KC], F32))
    modw = st0.enter_context(nc.sbuf_tensor("modw", [128, KC, 128], F32R))
    wmbuf = st0.enter_context(nc.sbuf_tensor("wmbuf", [128, 4, 512], F32R))
    p.do("gpsimd", lambda e: e.dma_start(out=identr[:], in_=eye),
         writes=["identr"], dma=True)
    p.do("sync", lambda e: e.dma_start(out=identf[:], in_=eye),
         writes=["identf"], dma=True)
    p.do("sync", lambda e: e.dma_start(out=toki[:].bitcast(F32), in_=toki_c),
         writes=["toki"], dma=True)
    for hh in range(NH):
        p.do("sync", lambda e, hh=hh: e.dma_start(
            out=erows[:, hh, :],
            in_=erow_c[hh:hh + 1].unsqueeze(1).broadcast_to([1, 128, NE])),
            writes=["erows"], dma=True)
    p.do("gpsimd", lambda e: e.memset(ones[:].bitcast(F32), 1.0), writes=["ones"])
    p.do("sync", lambda e: e.dma_start(out=kxc[:], in_=d_kx), writes=["kxc"], dma=True)
    p.do("sync", lambda e: e.dma_start(out=kzc[:], in_=d_kz), writes=["kzc"], dma=True)
    p.do("sync", lambda e: e.dma_start(out=aneg[:], in_=d_alog), writes=["aneg"], dma=True)
    p.do("sync", lambda e: e.dma_start(out=bdt2[:], in_=d_bdt), writes=["bdt2"], dma=True)
    p.do("sync", lambda e: e.dma_start(out=ddc[:], in_=d_dd), writes=["ddc"], dma=True)
    p.do("sync", lambda e: e.dma_start(out=silu_c[:], in_=d_c),
        writes=["silu_c"], dma=True)
    p.do("sync", lambda e: e.dma_start(
        out=gab[:], in_=d_bout.unsqueeze(1).broadcast_to([1, 128, D])),
        writes=["gab"], dma=True)
    p.do("gpsimd", lambda e: e.dma_start(
        out=wxp[:], in_=d_wxp.rearrange("(c q) n -> q c n", q=128)),
        writes=["wxp"], dma=True)
    p.do("gpsimd", lambda e: e.dma_start(out=wdt[:], in_=d_wdt), writes=["wdt"], dma=True)
    p.do("gpsimd", lambda e: e.dma_start(
        out=rwt[:], in_=d_rw.rearrange("(c q) n -> q c n", q=128)),
        writes=["rwt"], dma=True)
    p.do("vector", lambda e: e.memset(stl[:], 0.0), writes=["stl"])
    p.do("vector", lambda e: e.memset(cols[:, 0:1], 1e-5), writes=["c_eps"])
    p.do("vector", lambda e: e.memset(cols[:, 1:2], 1.0), writes=["c_one"])

    p.do("scalar", lambda e: e.activation(out=aneg[:], in_=aneg[:], func=AF.Exp),
         reads=["aneg"], writes=["aneg"])
    p.do("vector", lambda e: e.tensor_scalar(out=aneg[:], in0=aneg[:], scalar1=-1.0,
                                             scalar2=None, op0=OP.mult),
         reads=["aneg"], writes=["aneg"])
    p.do("vector", lambda e: e.tensor_scalar(out=bdt2[:], in0=bdt2[:], scalar1=2.0,
                                             scalar2=None, op0=OP.mult),
         reads=["bdt2"], writes=["bdt2"])
    p.do("scalar", lambda e: e.activation(out=silu_c[:], in_=silu_c[:], func=AF.Silu),
         reads=["silu_c"], writes=["silu_c"])
    settle("scalar", ["silu_c"])
    for c in range(KC):
        p.do("vector", lambda e, c=c: e.tensor_scalar(
            out=modw[:, c, :], in0=ones[:].bitcast(F32), scalar1=silu_c[:, c:c + 1],
            scalar2=None, op0=OP.mult), reads=["ones", "silu_c"], writes=["modw"])

    for nb in range(12):
        pi = nb % 2
        pv = pAB[pi][:, 0:512]
        b0, b1 = bankres(pi)
        for c in range(0, KC, 2):
            for cc in (c, c + 1):
                par = (nb * KC + cc) % 4
                p.do("gpsimd", lambda e, cc=cc, nb=nb, par=par: e.dma_start(
                    out=wmbuf[:, par, :],
                    in_=d_wmod[cc * 128:(cc + 1) * 128, nb * 512:(nb + 1) * 512]),
                    writes=["wmbuf%d" % par], dma=True, defer=True)
            p.dma_flush("gpsimd")
            for cc in (c, c + 1):
                par = (nb * KC + cc) % 4
                pre = (list(p.r.get(b0, ())) + [p.w.get(b0)]) if cc == 0 else []
                p.do("tensor", lambda e, cc=cc, par=par, pv=pv: e.matmul(
                    pv, modw[:, cc, :], wmbuf[:, par, :],
                    start=(cc == 0), stop=(cc == KC - 1)),
                    reads=["modw", "wmbuf%d" % par], extra=pre,
                    writes=([b0] if cc == KC - 1 else []))
        mt = modA if nb < 6 else modB
        mr = "modA" if nb < 6 else "modB"
        mo = (nb % 6) * 512
        p.do("vector", lambda e, mt=mt, mo=mo, pv=pv: e.tensor_copy(
            out=mt[:, mo:mo + 512], in_=pv),
            reads=[b0], writes=[mr])

    def modsl(i):
        t = modA if i < 3 else modB
        return t[:, (i % 3) * D:(i % 3 + 1) * D]

    def modres(i):
        return "modA" if i < 3 else "modB"

    for i in (1, 4):
        p.do("vector", lambda e, i=i: e.tensor_scalar(
            out=modsl(i), in0=modsl(i), scalar1=1.0, scalar2=None, op0=OP.add),
            reads=[modres(i)], writes=[modres(i)])
    for gi in (2, 5):
        p.do("vector", lambda e, gi=gi: e.tensor_scalar(
            out=sqscr[:], in0=modsl(gi), scalar1=-1.0, scalar2=None, op0=OP.mult),
            reads=[modres(gi)], writes=["sqscr"])
        p.do("vector", lambda e, gi=gi: e.tensor_tensor(
            out=sqscr[:], in0=sqscr[:], in1=modsl(gi), op=OP.max),
            reads=["sqscr", modres(gi)], writes=["sqscr"])
        p.do("vector", lambda e, gi=gi: e.tensor_scalar(
            out=sqscr[:], in0=sqscr[:], scalar1=0.25, scalar2=None, op0=OP.max),
            reads=["sqscr"], writes=["sqscr"])
        p.do("vector", lambda e, gi=gi: e.tensor_scalar(
            out=escr[:, 0, :], in0=modsl(gi), scalar1=0.0, scalar2=None, op0=OP.is_ge),
            reads=[modres(gi)], writes=["escr0"])
        p.do("vector", lambda e, gi=gi: e.tensor_scalar(
            out=escr[:, 0, :], in0=escr[:, 0, :], scalar1=2.0, scalar2=-1.0,
            op0=OP.mult, op1=OP.add), reads=["escr0"], writes=["escr0"])
        p.do("vector", lambda e, gi=gi: e.tensor_tensor(
            out=modsl(gi), in0=escr[:, 0, :], in1=sqscr[:], op=OP.mult),
            reads=["escr0", "sqscr"], writes=[modres(gi)])
    p.do("vector", lambda e: e.tensor_tensor(out=gab[:], in0=gab[:], in1=modsl(2),
                                             op=OP.mult),
         reads=["gab", "modA"], writes=["gab"])
    if debug:
        p.do("sync", lambda e: e.dma_start(out=dbg["mod"][:, 0:3 * D], in_=modA[:]),
             reads=["modA"], dma=True)
        p.do("sync", lambda e: e.dma_start(out=dbg["mod"][:, 3 * D:6 * D], in_=modB[:]),
             reads=["modB"], dma=True)

    p.do("gpsimd", lambda e: e.dma_start(out=d_winr, in_=d_win), writes=["winr"],
         dma=True, defer=True)
    p.do("gpsimd", lambda e: e.dma_start(out=d_woutr, in_=d_wout), writes=["woutr"],
         dma=True, defer=True)
    p.dma_flush("gpsimd")
    p.do("scalar", lambda e: e.dma_start(
        out=d_tok[:, 0].rearrange("(b a) -> b a", b=128),
        in_=stl[:, 0:100].bitcast(I32)),
        reads=["stl"], writes=["dtok0", "dtok1"], dma=True)
    st0.close()
    p.fence()
    for ei in range(NE):
        p.do("gpsimd", lambda e, ei=ei: e.dma_start(
            out=d_werb[ei], in_=d_we[ei]),
            writes=["werb%d" % ei], dma=True, defer=True)
    if stage < 1:
        return _emit(nc, p, None)

    def ln_batch(tiles, src_dram, sh_sl, sc_sl, hdst_fn_of, hdst_res, mod_res,
                 interleave=None, h2s=False):
        def cb(j):
            return 16 + 8 * j

        for j0 in range(0, len(tiles), 2):
            pair = [(j, tiles[j]) for j in range(j0, min(j0 + 2, len(tiles)))]
            for j, tt in pair:
                par = tt % 2
                p.do("sync", lambda e, tt=tt, par=par: e.dma_start(
                    out=xtile[:, par, :], in_=src_dram[tt * 128:(tt + 1) * 128, :]),
                    writes=["xtile%d" % par], dma=True, defer=True)
            p.dma_flush("sync")
            for j, tt in pair:
                par = tt % 2
                p.do("vector", lambda e, j=j, par=par: e.tensor_reduce(
                    out=cols[:, cb(j):cb(j) + 1], in_=xtile[:, par, :], axis=AX.X,
                    op=OP.add, negate=True), reads=["xtile%d" % par],
                    writes=["cL%d" % j])
                p.do("scalar", lambda e, j=j, par=par: e.activation(
                    out=sqscr[:], in_=xtile[:, par, :], func=AF.Square,
                    accum_out=cols[:, cb(j) + 2:cb(j) + 3]),
                    reads=["xtile%d" % par], writes=["sqscr", "cQ%d" % j])
        settle("vector", ["cL%d" % j for j in range(len(tiles))])
        for j in range(len(tiles)):
            p.do("scalar", lambda e, j=j: e.mul(
                cols[:, cb(j) + 1:cb(j) + 2], cols[:, cb(j):cb(j) + 1], 1.0 / D),
                reads=["cL%d" % j], writes=["cM%d" % j])
        for j in range(len(tiles)):
            p.do("scalar", lambda e, j=j: e.activation(
                out=cols[:, cb(j) + 3:cb(j) + 4], in_=cols[:, cb(j) + 1:cb(j) + 2],
                func=AF.Square), reads=["cM%d" % j], writes=["c2%d" % j])
        for j in range(len(tiles)):
            p.do("scalar", lambda e, j=j: e.mul(
                cols[:, cb(j) + 4:cb(j) + 5], cols[:, cb(j) + 3:cb(j) + 4], -1.0),
                reads=["c2%d" % j], writes=["c3%d" % j])
        for j in range(len(tiles)):
            p.do("scalar", lambda e, j=j: e.add(
                cols[:, cb(j) + 4:cb(j) + 5], cols[:, cb(j) + 4:cb(j) + 5],
                cols[:, 0:1]), reads=["c3%d" % j, "c_eps"], writes=["c3%d" % j])
        for j in range(len(tiles)):
            p.do("scalar", lambda e, j=j: e.activation(
                out=cols[:, cb(j) + 5:cb(j) + 6], in_=cols[:, cb(j) + 2:cb(j) + 3],
                func=AF.Ln, scale=1.0 / D, bias=cols[:, cb(j) + 4:cb(j) + 5]),
                reads=["cQ%d" % j, "c3%d" % j], writes=["cS%d" % j])
        for j in range(len(tiles)):
            p.do("scalar", lambda e, j=j: e.activation(
                out=cols[:, cb(j) + 6:cb(j) + 7], in_=cols[:, cb(j) + 5:cb(j) + 6],
                func=AF.Exp, scale=-0.5), reads=["cS%d" % j], writes=["cR%d" % j])
        for j, tt in enumerate(tiles):
            par = tt % 2
            hsl = tt % 2 if h2s else 0
            htm = htm2[:, hsl, :]
            hres = "htm%d" % hsl
            eres = "escr%d" % par
            xres = "xtile%d" % par
            p.do("sync", lambda e, tt=tt, par=par: e.dma_start(
                out=xtile[:, par, :], in_=src_dram[tt * 128:(tt + 1) * 128, :]),
                writes=[xres], dma=True)
            p.do("scalar", lambda e, j=j, par=par: e.add(
                escr[:, par, :], xtile[:, par, :], cols[:, cb(j) + 1:cb(j) + 2]),
                reads=[xres, "cM%d" % j], writes=[eres])
            p.do("vector", lambda e, j=j, par=par: e.scalar_tensor_tensor(
                out=escr[:, par, :], in0=escr[:, par, :],
                scalar=cols[:, cb(j) + 6:cb(j) + 7], in1=sc_sl,
                op0=OP.mult, op1=OP.mult),
                reads=[eres, "cR%d" % j, mod_res], writes=[eres])
            p.do("vector", lambda e, par=par, htm=htm: e.tensor_tensor(
                out=htm[:], in0=escr[:, par, :], in1=sh_sl, op=OP.add),
                reads=[eres, mod_res], writes=[hres])
            if h2s:
                p.do("scalar", lambda e, tt=tt, htm=htm: e.dma_start(
                    out=d_h2s[tt * 128:(tt + 1) * 128, :], in_=htm[:]),
                    reads=[hres], writes=["h2s"], dma=True)
            hdst_fn = hdst_fn_of(tt)
            for c in range(KC):
                trb = "ps3_0" if c % 2 == 0 else "ps3_1"
                trv = tr0 if c % 2 == 0 else tr1
                p.do("tensor", lambda e, c=c, trv=trv, htm=htm: e.transpose(
                    trv, htm[:, c * 128:(c + 1) * 128], identr[:]),
                    reads=[hres, "identr"], writes=[trb])
                dst, s0, w = hdst_fn(c)
                if c % 2 == 0:
                    p.do("vector", lambda e, dst=dst, trv=trv, s0=s0, w=w:
                         e.tensor_copy(out=dst, in_=trv.bitcast(F32)[:, s0:s0 + w]),
                         reads=[trb], writes=[hdst_res])
                else:
                    p.do("scalar", lambda e, dst=dst, trv=trv, s0=s0, w=w: e.copy(
                        dst, trv.bitcast(F32)[:, s0:s0 + w]),
                        reads=[trb], writes=[hdst_res])
            if interleave is not None:
                interleave(j)

    stA = contextlib.ExitStack()

    def qa(name, shape, dt):
        return stA.enter_context(nc.sbuf_tensor(name, shape, dt))

    hfm = qa("hfm", [128, KC, 528], F32R)
    rawb = qa("rawb", [128, 2, 520], F32)
    convs = qa("convs", [128, 2, TQ], F32)
    xh = qa("xh", [128, KC, TQ], F32R)
    zs = qa("zs", [128, KC, TQ], F32R)
    delta = qa("delta", [128, KC, TQ], F32)
    yq = qa("yq", [128, KC, TQ], F32R)
    dAs = qa("dAs", [128, 2, TQ], F32)
    bus = qa("bus", [128, 4, TQ], F32)
    hsc = qa("hsc", [128, 2, TQ], F32)
    bbc = qa("bbc", [128, 2, TQ], F32)
    cbc = qa("cbc", [128, 2, TQ], F32)
    xdbl = qa("xdbl", [128, 4, 96], F32R)
    dtT = qa("dtT", [DTR, TQ], F32R)
    bcT = qa("bcT", [32, TQ], F32R)
    winb = qa("winb", [128, 2, KC, 128], F32R)
    woutb = qa("woutb", [128, 2, D], F32R)

    NQ_RUN = NQ if stage >= 6 else 1
    wevac = rawb.ap().rearrange("p a b -> p (a b)")[:, 0:D]
    for q in range(NQ_RUN):
        qs = q * TQ
        qe = qs + TQ
        lo_tile = max(qs // 128 - 1, 0)
        hi_tile = min(qe // 128 + 1, NTT)
        win0 = qs - 8
        win1 = qe + 8

        def hdst_of(tt, win0=win0, win1=win1):
            a = max(tt * 128, win0)
            b = min(tt * 128 + 128, win1)
            off = a - win0
            s0 = a - tt * 128
            w = b - a

            def hdst(c, off=off, s0=s0, w=w):
                return hfm[:, c, off:off + w], s0, w
            return hdst
        ln_batch(list(range(lo_tile, hi_tile)), dx, modsl(0), modsl(1),
                 hdst_of, "hfm", "modA")

        if debug:
            for c in range(KC):
                p.do("sync", lambda e, c=c: e.dma_start(
                    out=dbg["hfm"][c, :, 8:520],
                    in_=hfm[:, c, 8:520].bitcast(F32)),
                    reads=["hfm"], dma=True)
        if stage < 2:
            stA.close()
            return _emit(nc, p, None)

        r0 = max(qs - 1, 0)
        r1 = min(qe + 2, T)
        L = r1 - r0
        c0 = r0 - win0
        LM = min(L, 512)
        LH = (L - LM + 1) // 2 * 2
        i0 = r0 - qs + 2
        if q == NQ - 1:
            for c in range(KC):
                p.do("vector", lambda e, c=c: e.memset(
                    hfm[:, c, 520:524].bitcast(F32), 0.0), writes=["hfm"])
        for m in range(16):
            par = m % 2
            pv = pAB[par]
            b0, b1 = bankres(par)
            p.do("sync", lambda e, m=m, par=par: e.dma_start(
                out=winb[:, par], in_=d_winr[:, m * 128:(m + 1) * 128].rearrange(
                    "(k q) mm -> q k mm", q=128)),
                reads=["winr"], writes=["winb%d" % par], dma=True)
            def win_mms(e, par=par, pv=pv, c0=c0, LM=LM, LH=LH):
                for k in range(KC):
                    mm = e.matmul(pv[:, 0:LM], winb[:, par, k, :],
                                  hfm[:, k, c0:c0 + LM],
                                  start=(k == 0), stop=(k == KC - 1))
                for k in range(KC):
                    mm = e.matmul(pv[:, 512:512 + LH], winb[:, par, k, :],
                                  hfm[:, k, c0 + LM:c0 + LM + LH],
                                  start=(k == 0), stop=(k == KC - 1))
                return mm
            p.do("tensor", win_mms, reads=["winb%d" % par, "hfm"], writes=[b0, b1])
            rres = "rawb%d" % par
            p.do("vector", lambda e, par=par, pv=pv, i0=i0, LM=LM: e.tensor_copy(
                out=rawb[:, par, i0:i0 + LM], in_=pv[:, 0:LM]),
                reads=[b0], writes=[rres])
            p.do("vector", lambda e, par=par, pv=pv, i0=i0, LM=LM, LH=LH: e.tensor_copy(
                out=rawb[:, par, i0 + LM:i0 + LM + LH], in_=pv[:, 512:512 + LH]),
                reads=[b1], writes=[rres])
            if q == 0:
                p.do("vector", lambda e, par=par: e.memset(rawb[:, par, 1:2], 0.0),
                     writes=[rres])
            if q == NQ - 1:
                p.do("vector", lambda e, par=par: e.memset(rawb[:, par, 514:517], 0.0),
                     writes=[rres])
            kcol = kxc if m < KC else kzc
            kres = "kxc" if m < KC else "kzc"
            ccol = (m % KC) * 4
            cres = "convs%d" % par
            p.do("vector", lambda e, par=par, kcol=kcol, ccol=ccol: e.tensor_scalar(
                out=convs[:, par, :], in0=rawb[:, par, 1:513],
                scalar1=kcol[:, ccol:ccol + 1], scalar2=None, op0=OP.mult),
                reads=[rres, kres], writes=[cres])
            for j in (1, 2, 3):
                p.do("vector", lambda e, par=par, kcol=kcol, ccol=ccol, j=j:
                     e.scalar_tensor_tensor(
                         out=convs[:, par, :], in0=rawb[:, par, 1 + j:513 + j],
                         scalar=kcol[:, ccol + j:ccol + j + 1], in1=convs[:, par, :],
                         op0=OP.mult, op1=OP.add),
                     reads=[rres, kres, cres], writes=[cres])
            dstt = xh[:, m % KC] if m < KC else zs[:, m % KC]
            dres = ("xh%d" % (m % KC)) if m < KC else ("zs%d" % (m % KC))
            p.do("scalar", lambda e, dstt=dstt, par=par: e.activation(
                out=dstt, in_=convs[:, par, :], func=AF.Silu),
                reads=[cres], writes=[dres])

        if debug:
            for c in range(KC):
                p.do("sync", lambda e, c=c, qs=qs, qe=qe: e.dma_start(
                    out=dbg["xh"][c, :, qs:qe], in_=xh[:, c].bitcast(F32)),
                    reads=["xh%d" % c], dma=True)
                p.do("sync", lambda e, c=c, qs=qs, qe=qe: e.dma_start(
                    out=dbg["z"][c, :, qs:qe], in_=zs[:, c].bitcast(F32)),
                    reads=["zs%d" % c], dma=True)
        if stage < 3:
            stA.close()
            return _emit(nc, p, None)
        for i in range(4):
            b0, b1 = bankres(2)

            def xp_mms(e, i=i):
                for c in range(KC):
                    mm = e.matmul(pC[:, 0:96], xh[:, c, i * 128:(i + 1) * 128],
                                  wxp[:, c, :], start=(c == 0), stop=(c == KC - 1))
                return mm
            p.do("tensor", xp_mms,
                 reads=["wxp"] + ["xh%d" % c for c in range(KC)], writes=[b0])
            p.do("vector", lambda e, i=i: e.tensor_copy(
                out=xdbl[:, i, :], in_=pC[:, 0:96]),
                reads=[b0], writes=["xdbl%d" % i])
        for i in range(4):
            p.do("tensor", lambda e, i=i: e.transpose(tr0[0:64, :], xdbl[:, i, 0:64], identr[:]),
                 reads=["xdbl%d" % i, "identr"], writes=["ps3_0"])
            p.do("vector", lambda e, i=i: e.tensor_copy(
                out=dtT[:, i * 128:(i + 1) * 128],
                in_=tr0.bitcast(F32)[0:64, :]), reads=["ps3_0"], writes=["dtT"])
            p.do("tensor", lambda e, i=i: e.transpose(tr1[0:32, :], xdbl[:, i, 64:96], identr[:]),
                 reads=["xdbl%d" % i, "identr"], writes=["ps3_1"])
            p.do("vector", lambda e, i=i: e.tensor_copy(
                out=bcT[:, i * 128:(i + 1) * 128],
                in_=tr1.bitcast(F32)[0:32, :]), reads=["ps3_1"], writes=["bcT"])

        for c in range(KC):
            half = c % 2
            pv = pC[:, 0:512] if half == 0 else pC[:, 512:1024]
            bn = bankres(2)[half]
            p.do("tensor", lambda e, c=c, pv=pv: e.matmul(
                pv, wdt[:, c * 128:(c + 1) * 128], dtT[:], start=True, stop=True),
                reads=["dtT", "wdt"], writes=[bn])
            p.do("scalar", lambda e, c=c, pv=pv: e.activation(
                out=convs[:, 0, :], in_=pv, func=AF.Exp, bias=bdt2[:, c:c + 1]),
                reads=[bn, "bdt2", "convs0"], writes=["convs0"])
            p.do("scalar", lambda e, c=c: e.activation(
                out=delta[:, c], in_=convs[:, 0, :], func=AF.Ln, bias=cols[:, 1:2]),
                reads=["convs0", "c_one"], writes=["delta%d" % c])
            p.do("vector", lambda e, c=c: e.tensor_scalar(
                out=yq[:, c], in0=xh[:, c].bitcast(F32),
                scalar1=ddc[:, c:c + 1], scalar2=None, op0=OP.mult),
                reads=["xh%d" % c, "ddc"], writes=["yq%d" % c])
            p.do("vector", lambda e, c=c: e.tensor_tensor(
                out=xh[:, c], in0=delta[:, c], in1=xh[:, c].bitcast(F32),
                op=OP.mult), reads=["delta%d" % c, "xh%d" % c], writes=["xh%d" % c])

        if debug:
            for c in range(KC):
                p.do("sync", lambda e, c=c, qs=qs, qe=qe: e.dma_start(
                    out=dbg["delta"][c, :, qs:qe], in_=delta[:, c]),
                    reads=["delta%d" % c], dma=True)
        if stage < 4:
            stA.close()
            return _emit(nc, p, None)
        def bc_dma(n):
            bp = n % 2
            p.do("sync", lambda e, n=n, bp=bp: e.dma_start(
                out=bbc[:, bp, :],
                in_=bcT.ap().bitcast(F32)[n:n + 1, :].unsqueeze(1)
                    .broadcast_to([1, 128, TQ])),
                reads=["bcT"], writes=["bbc%d" % bp], dma=True, defer=True)
            p.do("sync", lambda e, n=n, bp=bp: e.dma_start(
                out=cbc[:, bp, :],
                in_=bcT.ap().bitcast(F32)[16 + n:17 + n, :].unsqueeze(1)
                    .broadcast_to([1, 128, TQ])),
                reads=["bcT"], writes=["cbc%d" % bp], dma=True, defer=True)
            p.dma_flush("sync")
        bc_dma(0)
        for n in range(NST):
            if n + 1 < NST:
                bc_dma(n + 1)
            for c in range(0, KC, 2):
                i = n * KC + c
                pk = (i // 2) % 3
                if pk == 2:
                    buap = convs[:, 0:2, :]
                    bur = ["convs0", "convs1"]
                else:
                    buap = bus[:, 2 * pk:2 * pk + 2, :]
                    bur = ["bus%d" % (2 * pk), "bus%d" % (2 * pk + 1)]
                for dc in (0, 1):
                    cn = (c + dc) * NST + n
                    sp = (i + dc) % 2
                    p.do("scalar", lambda e, c=c, dc=dc, cn=cn, sp=sp: e.activation(
                        out=dAs[:, sp, :], in_=delta[:, c + dc], func=AF.Exp,
                        scale=aneg[:, cn:cn + 1]),
                        reads=["delta%d" % (c + dc), "aneg"], writes=["dAs%d" % sp])
                p.do("gpsimd", lambda e, c=c, buap=buap, bp=n % 2: e.tensor_tensor(
                    out=buap, in0=xh[:, c:c + 2].bitcast(F32),
                    in1=bbc[:, bp, :].unsqueeze(1).broadcast_to([128, 2, TQ]),
                    op=OP.mult),
                    reads=["xh%d" % c, "xh%d" % (c + 1), "bbc%d" % (n % 2)],
                    writes=bur)
                for dc in (0, 1):
                    cn = (c + dc) * NST + n
                    sp = (i + dc) % 2
                    init = 0.0 if q == 0 else hstash[:, cn * 4 + 3:cn * 4 + 4]
                    rds = ["dAs%d" % sp, bur[dc]]
                    if q > 0:
                        rds.append("hst%d" % cn)
                    p.do("vector", lambda e, sp=sp, buap=buap, dc=dc, init=init:
                         e.tensor_tensor_scan(
                             out=hsc[:, sp, :], data0=dAs[:, sp, :],
                             data1=buap[:, dc, :],
                             initial=init, op0=OP.mult, op1=OP.add),
                         reads=rds, writes=["hsc%d" % sp])
                    p.do("scalar", lambda e, sp=sp, cn=cn: e.copy(
                        hstash[:, cn * 4:cn * 4 + 4], hsc[:, sp, TQ - 4:TQ]),
                        reads=["hsc%d" % sp], writes=["hst%d" % cn])
                p.do("vector", lambda e, buap=buap, bp=n % 2: e.tensor_tensor(
                    out=buap, in0=hsc[:, :, :],
                    in1=cbc[:, bp, :].unsqueeze(1).broadcast_to([128, 2, TQ]),
                    op=OP.mult),
                    reads=["hsc0", "hsc1", "cbc%d" % (n % 2)],
                    writes=bur)
                p.do("vector", lambda e, c=c, buap=buap: e.tensor_tensor(
                    out=yq[:, c:c + 2], in0=yq[:, c:c + 2].bitcast(F32),
                    in1=buap, op=OP.add),
                    reads=["yq%d" % c, "yq%d" % (c + 1)] + bur,
                    writes=["yq%d" % c, "yq%d" % (c + 1)])

        if debug:
            for c in range(KC):
                p.do("sync", lambda e, c=c, qs=qs, qe=qe: e.dma_start(
                    out=dbg["y"][c, :, qs:qe], in_=yq[:, c].bitcast(F32)),
                    reads=["yq%d" % c], dma=True)

        if stage < 5:
            stA.close()
            return _emit(nc, p, None)
        for k in range(16):
            par = k % 2
            src = yq if k < KC else zs
            sres = ("yq%d" % (k % KC)) if k < KC else ("zs%d" % (k % KC))
            cc = k % KC
            p.do("scalar", lambda e, k=k, par=par: e.dma_start(
                out=woutb[:, par, :], in_=d_woutr[k * 128:(k + 1) * 128, :]),
                reads=["woutr"], writes=["woutb%d" % par], dma=True)
            def wo_mms(e, k=k, par=par, src=src, cc=cc):
                for i in range(4):
                    for nn in range(2):
                        mm = e.matmul(pAB[i][:, nn * 512:(nn + 1) * 512],
                                      src[:, cc, i * 128:(i + 1) * 128],
                                      woutb[:, par, nn * 512:(nn + 1) * 512],
                                      start=(k == 0), stop=(k == 15))
                return mm
            allb = [bn for i in range(4) for bn in bankres(i)]
            p.do("tensor", wo_mms, reads=["woutb%d" % par, sres],
                 writes=(allb if k == 0 else allb))
        for i in range(4):
            tt = qs // 128 + i
            par = i % 2
            b0, b1 = bankres(i)
            xres = "xtile%d" % par
            p.do("sync", lambda e, tt=tt, par=par: e.dma_start(
                out=xtile[:, par, :], in_=dx[tt * 128:(tt + 1) * 128, :]),
                writes=[xres], dma=True)
            p.do("vector", lambda e, i=i: e.tensor_tensor(
                out=wevac, in0=pAB[i][:], in1=modsl(2), op=OP.mult),
                reads=[b0, b1, "modA", "rawb0", "rawb1"],
                writes=["rawb0", "rawb1"])
            p.do("vector", lambda e: e.tensor_tensor(
                out=wevac, in0=wevac, in1=gab[:], op=OP.add),
                reads=["rawb0", "rawb1", "gab"], writes=["rawb0", "rawb1"])
            p.do("vector", lambda e, par=par: e.tensor_tensor(
                out=xtile[:, par, :], in0=xtile[:, par, :], in1=wevac,
                op=OP.add), reads=[xres, "rawb0", "rawb1"], writes=[xres])
            p.do("sync", lambda e, tt=tt, par=par: e.dma_start(
                out=d_x2s[tt * 128:(tt + 1) * 128, :], in_=xtile[:, par, :]),
                reads=[xres], writes=["x2s"], dma=True)
            if debug:
                p.do("sync", lambda e, tt=tt, par=par: e.dma_start(
                    out=dbg["x2"][tt * 128:(tt + 1) * 128, :], in_=xtile[:, par, :]),
                    reads=[xres], dma=True)

    stA.close()
    stE.close()
    p.fence()
    if stage < 7:
        return _emit(nc, p, None)

    stB = contextlib.ExitStack()

    def ba(name, shape, dt):
        return stB.enter_context(nc.sbuf_tensor(name, shape, dt))

    h2fm = ba("h2fm", [128, KC, TH], F32R)
    web2 = ba("web2", [128, 2, KC, D], BF16)
    lhsT = ba("lhsT", [128, 2, KC, CAP], BF16)
    gbA = ba("gbA", [128, 2, D], F32R)
    gbB = ba("gbB", [64, 2, D], F32R)
    gct = ba("gct", [128, 2, 2, D], F32)
    wmT = ba("wmT", [NE, TH], F32)
    ztE = ba("ztE", [NE, TH], F32)
    ppf = ba("ppf", [NE, TH], F32)
    ptokb = ba("ptokb", [128, KC, NE], F32)
    growb = ba("growb", [128, KC, NE], F32)
    selb = ba("selb", [128, KC, NE], F32)
    scrb = ba("scrb", [128, KC, NE], F32)
    colsb = ba("colsb", [128, KC, 8], F32)
    cidx = ba("cidx", [128, NTT, 2], I32)
    wab = ba("wab", [128, NTT, 2], F32)
    gi32 = ba("gi32", [128, 2, 2], I32)
    ctrb = ba("ctrb", [128, 2, D], F32)
    ctrb1 = ba("ctrb1", [64, 2, D], F32)
    p.dma_flush("gpsimd")
    p.do("vector", lambda e: e.memset(ztE[:], 0.0), writes=["ztE"])

    for pe_ in (0, 1):
        p.do("sync" if pe_ == 0 else "scalar", lambda e, ei=pe_, par=pe_:
             e.dma_start(out=web2[:, par], in_=d_werb[ei:ei + 1].rearrange(
                 "o (c q) n -> q (o c) n", q=128)),
             reads=["werb%d" % pe_], writes=["web%d" % pe_], dma=True)

    def emit_ln_router(hh):
        hs0 = hh * TH

        def h2dst_of(tt, hs0=hs0):
            off = (tt - hs0 // 128) * 128

            def hdst(c, off=off):
                return h2fm[:, c, off:off + 128], 0, 128
            return hdst
        tiles = [hs0 // 128 + i for i in range(8)]
        ln_batch(tiles, d_x2s, modsl(3), modsl(4), h2dst_of, "h2fm", "modB",
                 h2s=True)

        RB = [("ps2_0", pC[:, 0:NE]), ("ps2_1", pC[:, 512:512 + NE]),
              ("ps3_0", pD[:, 0:NE]), ("ps3_1", pD[:, 512:512 + NE])]
        for g in range(2):
            idxs = list(range(g * 4, (g + 1) * 4))
            for j, i in enumerate(idxs):
                off = i * 128
                bn, pv = RB[j]

                def rt_mms(e, off=off, pv=pv):
                    for c in range(KC):
                        mm = e.matmul(pv, h2fm[:, c, off:off + 128],
                                      rwt[:, c, :], start=(c == 0),
                                      stop=(c == KC - 1))
                    return mm
                p.do("tensor", rt_mms, reads=["h2fm", "rwt"], writes=[bn])
            for j, i in enumerate(idxs):
                bn, pv = RB[j]
                p.do("vector", lambda e, j=j, pv=pv: e.tensor_reduce(
                    out=cols[:, 80 + j:81 + j], in_=pv, axis=AX.X, op=OP.max,
                    negate=True), reads=[bn], writes=["cX%d" % j])
            settle("vector", ["cX%d" % j for j in range(4)])
            for j, i in enumerate(idxs):
                tt = hs0 // 128 + i
                bn, pv = RB[j]
                p.do("scalar", lambda e, j=j, tt=tt, pv=pv: e.activation(
                    out=wmsk[:, tt, :], in_=pv, func=AF.Exp,
                    bias=cols[:, 80 + j:81 + j], accum_out=cols[:, 84 + j:85 + j]),
                    reads=[bn, "cX%d" % j], writes=["wm%d" % tt, "cY%d" % j])
            settle("scalar", ["cY%d" % j for j in range(4)] +
                   ["wm%d" % (hs0 // 128 + i) for i in idxs])
            for j, i in enumerate(idxs):
                p.do("scalar", lambda e, j=j: e.activation(
                    out=cols[:, 88 + j:89 + j], in_=cols[:, 84 + j:85 + j],
                    func=AF.Ln), reads=["cY%d" % j], writes=["cZ%d" % j])
            for j, i in enumerate(idxs):
                tt = hs0 // 128 + i
                p.do("scalar", lambda e, j=j, tt=tt: e.activation(
                    out=recipc[:, tt:tt + 1], in_=cols[:, 88 + j:89 + j],
                    func=AF.Exp, scale=-1.0), reads=["cZ%d" % j], writes=["recip"])
            for j, i in enumerate(idxs):
                tt = hs0 // 128 + i
                p.do("vector", lambda e, j=j, tt=tt: e.max(
                    out=cols[:, 92 + 8 * j:100 + 8 * j], in_=wmsk[:, tt, :]),
                    reads=["wm%d" % tt], writes=["c8%d" % j])
            settle("vector", ["c8%d" % j for j in range(4)])
            for j, i in enumerate(idxs):
                tt = hs0 // 128 + i
                p.do("vector", lambda e, j=j, tt=tt: e.tensor_scalar(
                    out=cols[:, 128 + 32 * j:160 + 32 * j], in0=wmsk[:, tt, :],
                    scalar1=cols[:, 93 + 8 * j:94 + 8 * j],
                    scalar2=None, op0=OP.is_ge),
                    reads=["wm%d" % tt, "c8%d" % j], writes=["cK%d" % j])
            settle("vector", ["cK%d" % j for j in range(4)])
            for j, i in enumerate(idxs):
                tt = hs0 // 128 + i
                p.do("vector", lambda e, j=j, tt=tt: e.tensor_tensor(
                    out=wmsk[:, tt, :], in0=wmsk[:, tt, :],
                    in1=cols[:, 128 + 32 * j:160 + 32 * j],
                    op=OP.mult), reads=["wm%d" % tt, "cK%d" % j], writes=["wm%d" % tt])
        if debug:
            for i in range(8):
                tt = hs0 // 128 + i
                off = i * 128
                p.do("sync", lambda e, tt=tt: e.dma_start(
                    out=dbg["wm"][:, tt, :], in_=wmsk[:, tt, :]),
                    reads=["wm%d" % tt], dma=True)
                for c in range(KC):
                    p.do("sync", lambda e, c=c, off=off, hs0=hs0: e.dma_start(
                        out=dbg["h2"][c, :, hs0 + off:hs0 + off + 128],
                        in_=h2fm[:, c, off:off + 128].bitcast(F32)),
                        reads=["h2fm"], dma=True)

    def emit_index(hh):
        hb = hh * KC
        for j in range(KC):
            tt = hb + j
            bn = "ps2_0" if j < 4 else "ps2_1"
            pvw = pC[0:NE, j * 128:(j + 1) * 128]
            p.do("tensor", lambda e, tt=tt, pvw=pvw: e.transpose(
                pvw, wmsk[:, tt, :], identf[:]),
                reads=["wm%d" % tt, "identf"], writes=[bn])
        for j in range(KC):
            bn = "ps2_0" if j < 4 else "ps2_1"
            pvw = pC[0:NE, j * 128:(j + 1) * 128]
            eng2 = "vector" if j % 2 == 0 else "scalar"
            if eng2 == "vector":
                p.do(eng2, lambda e, j=j, pvw=pvw: e.tensor_copy(
                    out=wmT[:, j * 128:(j + 1) * 128], in_=pvw),
                    reads=[bn], writes=["wmT"])
            else:
                p.do(eng2, lambda e, j=j, pvw=pvw: e.copy(
                    wmT[:, j * 128:(j + 1) * 128], pvw),
                    reads=[bn], writes=["wmT"])
        p.do("vector", lambda e: e.tensor_scalar(
            out=wmT[:], in0=wmT[:], scalar1=0.0, scalar2=None, op0=OP.is_gt),
            reads=["wmT"], writes=["wmT"])
        p.do("vector", lambda e: e.tensor_tensor_scan(
            out=ppf[:], data0=ztE[:], data1=wmT[:], initial=0.0,
            op0=OP.add, op1=OP.add), reads=["ztE", "wmT"], writes=["ppf"])
        for j in range(KC):
            p.do("tensor", lambda e, j=j: e.transpose(
                pD[0:128, 256 + j * NE:256 + (j + 1) * NE],
                ppf[:, j * 128:(j + 1) * 128], identf[0:NE, 0:NE]),
                reads=["ppf", "identf"], writes=["ps3_0"])
        for j in range(KC):
            eng2 = "vector" if j % 2 == 0 else "scalar"
            pvw = pD[0:128, 256 + j * NE:256 + (j + 1) * NE]
            if eng2 == "vector":
                p.do(eng2, lambda e, j=j, pvw=pvw: e.tensor_copy(
                    out=ptokb[:, j, :], in_=pvw), reads=["ps3_0"], writes=["ixs"])
            else:
                p.do(eng2, lambda e, j=j, pvw=pvw: e.copy(
                    ptokb[:, j, :], pvw), reads=["ps3_0"], writes=["ixs"])
        for j in range(KC):
            p.do("vector", lambda e, j=j: e.tensor_scalar(
                out=ptokb[:, j, :], in0=ptokb[:, j, :], scalar1=-1.0,
                scalar2=float(CAP - 1), op0=OP.add, op1=OP.min),
                reads=["ixs"], writes=["ixs"])
        for j in range(KC):
            p.do("vector", lambda e, j=j, hh=hh: e.tensor_tensor(
                out=growb[:, j, :], in0=ptokb[:, j, :], in1=erows[:, hh, :],
                op=OP.add), reads=["ixs", "erows"], writes=["ixs"])
        for j in range(KC):
            tt = hb + j
            p.do("vector", lambda e, j=j, tt=tt: e.tensor_scalar(
                out=selb[:, j, :], in0=wmsk[:, tt, :], scalar1=0.0,
                scalar2=None, op0=OP.is_gt),
                reads=["wm%d" % tt], writes=["ixs"])
        for j in range(KC):
            p.do("vector", lambda e, j=j: e.tensor_scalar(
                out=scrb[:, j, :], in0=growb[:, j, :], scalar1=-1.0,
                scalar2=BIGC, op0=OP.mult, op1=OP.add),
                reads=["ixs"], writes=["ixs"])
        for j in range(KC):
            p.do("vector", lambda e, j=j: e.tensor_tensor(
                out=scrb[:, j, :], in0=scrb[:, j, :], in1=selb[:, j, :],
                op=OP.mult), reads=["ixs"], writes=["ixs"])
        for j in range(KC):
            p.do("vector", lambda e, j=j: e.tensor_reduce(
                out=colsb[:, j, 0:1], in_=scrb[:, j, :], axis=AX.X,
                op=OP.max), reads=["ixs"], writes=["ixs"])
        for j in range(KC):
            p.do("vector", lambda e, j=j: e.tensor_scalar(
                out=scrb[:, j, :], in0=growb[:, j, :], scalar1=1.0,
                scalar2=None, op0=OP.add), reads=["ixs"], writes=["ixs"])
        for j in range(KC):
            p.do("vector", lambda e, j=j: e.tensor_tensor(
                out=scrb[:, j, :], in0=scrb[:, j, :], in1=selb[:, j, :],
                op=OP.mult), reads=["ixs"], writes=["ixs"])
        for j in range(KC):
            p.do("vector", lambda e, j=j: e.tensor_reduce(
                out=colsb[:, j, 1:2], in_=scrb[:, j, :], axis=AX.X,
                op=OP.max), reads=["ixs"], writes=["ixs"])
        for j in range(KC):
            p.do("vector", lambda e, j=j: e.tensor_scalar(
                out=colsb[:, j, 2:3], in0=colsb[:, j, 0:1], scalar1=-1.0,
                scalar2=BIGC, op0=OP.mult, op1=OP.add),
                reads=["ixs"], writes=["ixs"])
        for j in range(KC):
            p.do("vector", lambda e, j=j: e.tensor_scalar(
                out=colsb[:, j, 3:4], in0=colsb[:, j, 1:2], scalar1=-1.0,
                scalar2=None, op0=OP.add), reads=["ixs"], writes=["ixs"])
        for j in range(KC):
            tt = hb + j
            p.do("vector", lambda e, j=j, tt=tt: e.tensor_copy(
                out=cidx[:, tt, :], in_=colsb[:, j, 2:4]),
                reads=["ixs"], writes=["cidx%d" % tt])
        for j in range(KC):
            p.do("vector", lambda e, j=j: e.tensor_scalar(
                out=scrb[:, j, :], in0=growb[:, j, :],
                scalar1=colsb[:, j, 2:3], scalar2=None, op0=OP.is_equal),
                reads=["ixs"], writes=["ixs"])
        for j in range(KC):
            tt = hb + j
            p.do("vector", lambda e, j=j, tt=tt: e.tensor_tensor(
                out=scrb[:, j, :], in0=scrb[:, j, :], in1=wmsk[:, tt, :],
                op=OP.mult), reads=["ixs", "wm%d" % tt], writes=["ixs"])
        for j in range(KC):
            tt = hb + j
            p.do("vector", lambda e, j=j, tt=tt: e.tensor_reduce(
                out=wab[:, tt, 0:1], in_=scrb[:, j, :], axis=AX.X,
                op=OP.add), reads=["ixs"], writes=["wab%d" % tt])
        for j in range(KC):
            p.do("vector", lambda e, j=j: e.tensor_scalar(
                out=scrb[:, j, :], in0=growb[:, j, :],
                scalar1=colsb[:, j, 3:4], scalar2=None, op0=OP.is_equal),
                reads=["ixs"], writes=["ixs"])
        for j in range(KC):
            tt = hb + j
            p.do("vector", lambda e, j=j, tt=tt: e.tensor_tensor(
                out=scrb[:, j, :], in0=scrb[:, j, :], in1=wmsk[:, tt, :],
                op=OP.mult), reads=["ixs", "wm%d" % tt], writes=["ixs"])
        for j in range(KC):
            tt = hb + j
            p.do("vector", lambda e, j=j, tt=tt: e.tensor_reduce(
                out=wab[:, tt, 1:2], in_=scrb[:, j, :], axis=AX.X,
                op=OP.add), reads=["ixs"], writes=["wab%d" % tt])
        for j in range(KC):
            tt = hb + j
            for k in range(2):
                p.do("gpsimd", lambda e, tt=tt, k=k:
                     e.indirect_dma_start(
                         out=d_tok, out_offset=IOA(cidx[:, tt, k:k + 1], 0),
                         in_=toki[:, tt:tt + 1], in_offset=None),
                     reads=["cidx%d" % tt, "toki", "dtok%d" % hh],
                     dma=True, defer=True)
        for k2 in range(2):
            p.pending_dma.setdefault("gpsimd", []).append(("w", "dtok%d" % hh))
        p.dma_flush("gpsimd")

    def emit_experts(hh):
        base = hh * 6400

        def rbk(ei):
            par = ei % 2
            p.do("gpsimd", lambda e, ei=ei, par=par: e.dma_start(
                out=gi32[:, par, 0:1],
                in_=d_tok[base + ei * CAP:base + ei * CAP + 128, :]),
                reads=["dtok%d" % hh], writes=["gi%d" % par], dma=True,
                defer=True)
            p.do("gpsimd", lambda e, ei=ei, par=par: e.dma_start(
                out=gi32[0:64, par, 1:2],
                in_=d_tok[base + ei * CAP + 128:base + ei * CAP + 192, :]),
                reads=["dtok%d" % hh], writes=["gi%d" % par], dma=True,
                defer=True)

        def gth(ei):
            par = ei % 2
            p.do("gpsimd", lambda e, par=par: e.indirect_dma_start(
                out=gbA[:, par, :], out_offset=None, in_=d_h2s,
                in_offset=IOA(gi32[:, par, 0:1], 0)),
                reads=["gi%d" % par, "h2s"], writes=["gbA%d" % par],
                dma=True, defer=True)
            p.do("gpsimd", lambda e, par=par: e.indirect_dma_start(
                out=gbB[:, par, :], out_offset=None, in_=d_h2s,
                in_offset=IOA(gi32[0:64, par, 1:2], 0)),
                reads=["gi%d" % par, "h2s"], writes=["gbB%d" % par],
                dma=True, defer=True)
        def trans(ei):
            par = ei % 2
            for c in range(KC):
                sbase = 0 if c % 2 == 0 else 512
                bn = "ps2_0" if c % 2 == 0 else "ps2_1"
                slot = pC[:, sbase:sbase + CAP]
                slotr = slot.bitcast(F32R)
                p.do("tensor", lambda e, c=c, par=par, slotr=slotr: e.transpose(
                    slotr[:, 0:128], gbA[:, par, c * 128:(c + 1) * 128],
                    identr[:]),
                    reads=["gbA%d" % par, "identr"], writes=[bn])
                p.do("tensor", lambda e, c=c, par=par, slotr=slotr: e.transpose(
                    slotr[:, 128:192], gbB[0:64, par, c * 128:(c + 1) * 128],
                    identr[0:64, 0:64]),
                    reads=["gbB%d" % par, "identr"], writes=[bn])
                p.do("vector", lambda e, c=c, par=par, slot=slot: e.tensor_copy(
                    out=lhsT[:, par, c, :], in_=slot),
                    reads=[bn], writes=["lhs%d" % par])
        rbk(0)
        p.dma_flush("gpsimd")
        gth(0)
        rbk(1)
        p.dma_flush("gpsimd")
        gth(1)
        rbk(2)
        p.dma_flush("gpsimd")
        trans(0)
        for ei in range(NE):
            par = ei % 2
            qeng = "sync" if par == 0 else "scalar"
            wres = "web%d" % par
            if ei + 2 < NE:
                gth(ei + 2)
            if ei + 3 < NE:
                rbk(ei + 3)
            p.dma_flush("gpsimd")
            if ei + 1 < NE:
                trans(ei + 1)
            pblk = pAB[0] if par == 0 else pAB[3]
            bka, bkb = bankres(0 if par == 0 else 3)
            b1a, b1b = bankres(1)

            def fe_mms(e, par=par, pblk=pblk):
                for c in range(KC):
                    for nn in range(2):
                        e.matmul(pblk[:, nn * 512:(nn + 1) * 512],
                                 lhsT[:, par, c, 0:128],
                                 web2[:, par, c, nn * 512:(nn + 1) * 512],
                                 start=(c == 0), stop=(c == KC - 1))
                for c in range(KC):
                    for nn in range(2):
                        mm = e.matmul(pB[0:64, nn * 512:(nn + 1) * 512],
                                      lhsT[:, par, c, 128:192],
                                      web2[:, par, c, nn * 512:(nn + 1) * 512],
                                      start=(c == 0), stop=(c == KC - 1))
                return mm
            p.do("tensor", fe_mms, reads=["lhs%d" % par, wres],
                 writes=[bka, bkb, b1a, b1b])
            wei = ei + 2
            if wei < NE or hh == 0:
                wsrc = wei if wei < NE else wei - NE
                p.do(qeng, lambda e, wsrc=wsrc, par=par: e.dma_start(
                    out=web2[:, par],
                    in_=d_werb[wsrc:wsrc + 1].rearrange(
                        "o (c q) n -> q (o c) n", q=128)),
                    reads=["werb%d" % wsrc], writes=[wres], dma=True,
                    defer=True)
            ctres = "ctr%s%d" % ("E" if par == 0 else "O", hh)
            p.do("vector", lambda e, par=par, pblk=pblk: e.tensor_copy(
                out=ctrb[:, par, :], in_=pblk[:]),
                reads=[bka, bkb], writes=["ctrb%d" % par])
            p.do("vector", lambda e, par=par: e.tensor_copy(
                out=ctrb1[:, par, :], in_=pB[0:64, :]),
                reads=[b1a, b1b], writes=["ctrb1%d" % par])
            p.do(qeng, lambda e, ei=ei, par=par: e.dma_start(
                out=d_ctr[base + ei * CAP:base + ei * CAP + 128, :],
                in_=ctrb[:, par, :]),
                reads=["ctrb%d" % par], writes=[ctres], dma=True, defer=True)
            p.do(qeng, lambda e, ei=ei, par=par: e.dma_start(
                out=d_ctr[base + ei * CAP + 128:base + ei * CAP + 192, :],
                in_=ctrb1[:, par, :]),
                reads=["ctrb1%d" % par], writes=[ctres], dma=True, defer=True)
            p.dma_flush(qeng)

    def emit_combine(hh):
        for i in range(8):
            tt = hh * KC + i
            tp = i % 2
            xres = "xtile%d" % tp
            gres = "gc%d" % tp
            for k in range(2):
                p.do("gpsimd", lambda e, tt=tt, tp=tp, k=k:
                     e.indirect_dma_start(
                         out=gct[:, tp, k, :], out_offset=None, in_=d_ctr,
                         in_offset=IOA(cidx[:, tt, k:k + 1], 0)),
                     reads=["cidx%d" % tt, "ctrE%d" % hh, "ctrO%d" % hh],
                     writes=[gres], dma=True, defer=True)
            p.dma_flush("gpsimd")
            p.do("sync", lambda e, tt=tt, tp=tp: e.dma_start(
                out=xtile[:, tp, :], in_=d_x2s[tt * 128:(tt + 1) * 128, :]),
                reads=["x2s"], writes=[xres], dma=True)
            p.do("vector", lambda e, tt=tt, tp=tp: e.tensor_scalar(
                out=escr[:, tp, :], in0=gct[:, tp, 0, :],
                scalar1=wab[:, tt, 0:1], scalar2=None, op0=OP.mult),
                reads=[gres, "wab%d" % tt], writes=["escr%d" % tp])
            p.do("vector", lambda e, tt=tt, tp=tp: e.scalar_tensor_tensor(
                out=escr[:, tp, :], in0=gct[:, tp, 1, :],
                scalar=wab[:, tt, 1:2], in1=escr[:, tp, :],
                op0=OP.mult, op1=OP.add),
                reads=[gres, "wab%d" % tt, "escr%d" % tp], writes=["escr%d" % tp])
            p.do("vector", lambda e, tt=tt, tp=tp: e.scalar_tensor_tensor(
                out=escr[:, tp, :], in0=escr[:, tp, :],
                scalar=recipc[:, tt:tt + 1], in1=modsl(5),
                op0=OP.mult, op1=OP.mult),
                reads=["escr%d" % tp, "recip", "modB"], writes=["escr%d" % tp])
            p.do("vector", lambda e, tp=tp: e.tensor_tensor(
                out=xtile[:, tp, :], in0=xtile[:, tp, :], in1=escr[:, tp, :],
                op=OP.add), reads=[xres, "escr%d" % tp], writes=[xres])
            p.do("sync", lambda e, tt=tt, tp=tp: e.dma_start(
                out=d_out[tt * 128:(tt + 1) * 128, :], in_=xtile[:, tp, :]),
                reads=[xres], writes=["outd"], dma=True)

    emit_ln_router(0)
    emit_index(0)
    if stage < 8:
        stB.close()
        return _emit(nc, p, None)
    emit_ln_router(1)
    emit_index(1)
    emit_experts(0)
    emit_combine(0)
    emit_experts(1)
    emit_combine(1)

    stB.close()
    return _emit(nc, p, p.w["outd"])


def _emit(nc, p, final_out):
    with (
        nc.semaphore("s_sync") as s_sync,
        nc.semaphore("s_gp") as s_gp,
        nc.semaphore("s_sdl") as s_sdl,
        nc.semaphore("s_gdl") as s_gdl,
        nc.semaphore("s_scd") as s_scd,
        nc.semaphore("s_ved") as s_ved,
        nc.semaphore("s_v") as s_v,
        nc.semaphore("s_s") as s_s,
        nc.semaphore("s_t") as s_t,
        nc.Block() as block,
    ):
        sems = {"sync": s_sync, "gpsimd": s_gp, "syncdl": s_sdl, "gpdl": s_gdl,
                "scdl": s_scd, "vedl": s_ved, "vector": s_v, "scalar": s_s,
                "tensor": s_t}

        def runner(name, final_waits=()):
            def _run(eng):
                for ws, fn, inc, sem in p.ops[name]:
                    for pe, val in ws:
                        eng.wait_ge(sems[pe], val)
                    if fn is None:
                        eng.sem_inc(sems[sem], inc)
                    else:
                        inst = fn(eng)
                        inst.then_inc(sems[sem], inc)
                for pe, val in final_waits:
                    if pe != name:
                        eng.wait_ge(sems[pe], val)
            return _run

        fw = ([final_out] if final_out else []) + [(e, p.cnt[e]) for e in SEMS]
        block.sync(runner("sync", final_waits=fw))
        block.gpsimd(runner("gpsimd"))
        block.vector(runner("vector"))
        block.scalar(runner("scalar"))
        block.tensor(runner("tensor"))

    return nc


def stage_inputs(inputs):
    f = np.ascontiguousarray
    x = inputs["L0_attn_x"]
    kx = inputs["L0_attn_conv_x_k"][:, 0, :].T.reshape(KC, 128, 4).transpose(1, 0, 2).reshape(128, 32)
    kz = inputs["L0_attn_conv_z_k"][:, 0, :].T.reshape(KC, 128, 4).transpose(1, 0, 2).reshape(128, 32)
    alog = inputs["L0_attn_A_log"].reshape(KC, 128, NST).transpose(1, 0, 2).reshape(128, 128)
    bdt = inputs["L0_attn_b_dt"].reshape(KC, 128).T
    dd = inputs["L0_attn_D"].reshape(KC, 128).T
    rwT = inputs["L0_ffn_router_w"].T
    common = dict(
        w_in=f(inputs["L0_attn_W_in"]),
        kx=f(kx), kz=f(kz),
        w_xproj=f(inputs["L0_attn_W_xproj"]),
        w_dt=f(inputs["L0_attn_W_dt"]),
        b_dt=f(bdt), a_log=f(alog), d_d=f(dd),
        w_out=f(inputs["L0_attn_W_out"]),
        b_out=f(inputs["L0_attn_b_out"].reshape(1, D)),
        r_wT=f(rwT),
        w_e=f(inputs["L0_ffn_expert_weights"]),
        wmod=f(inputs["L0_wmod"]),
    )
    maps = []
    for b in range(8):
        m = dict(common)
        m["x"] = f(np.asarray(x[b]))
        m["c_row"] = f(np.asarray(inputs["L0_c"][b]).reshape(KC, 128).T)
        maps.append(m)
    return maps


def kernel(**inputs):
    maps = stage_inputs(inputs)
    nc = build_program(debug=False)
    res = run_bass_kernel_spmd(nc, maps, core_ids=list(range(8)))
    return np.stack([np.asarray(r["out"]) for r in res.results], axis=0)


# version 17
# speedup vs baseline: 1.0651x; 1.0651x over previous
import contextlib
import numpy as np
import concourse.bass as bass
import concourse.mybir as mybir
from concourse.bass_utils import run_bass_kernel_spmd

F32 = mybir.dt.float32
F32R = mybir.dt.float32r
BF16 = mybir.dt.bfloat16
I32 = mybir.dt.int32
IOA = bass.IndirectOffsetOnAxis
AF = mybir.ActivationFunctionType
OP = mybir.AluOpType
AX = mybir.AxisListType

T = 2048
D = 1024
DIN = 2048
NST = 16
DTR = 64
NE = 32
NQ = 4
TQ = T // NQ
NH = 2
TH = T // NH
NTT = T // 128
KC = D // 128
CAP = 192
BIGC = 16384.0

ENGS = ("sync", "gpsimd", "vector", "scalar", "tensor")
SEMS = ("sync", "gpsimd", "vector", "scalar", "tensor", "syncdl", "gpdl", "scdl", "vedl")
DLSEM = {"sync": "syncdl", "gpsimd": "gpdl", "scalar": "scdl", "vector": "vedl"}


class Plan:

    def __init__(self):
        self.ops = {e: [] for e in ENGS}
        self.cnt = {e: 0 for e in SEMS}
        self.lastwait = {e: {} for e in ENGS}
        self.w = {}
        self.r = {}
        self.pending_fence = {}
        self.pending_dma = {}

    def fence(self):
        toks = [(e, self.cnt[e]) for e in SEMS if self.cnt[e] > 0]
        for e in ENGS:
            self.pending_fence[e] = list(toks)

    def _op(self, eng, sem, fn, waits, inc):
        if self.pending_fence.get(eng):
            waits = list(waits) + self.pending_fence.pop(eng)
        ws = []
        for tok in waits:
            if tok is None:
                continue
            pe, val = tok
            if pe == eng:
                continue
            if self.lastwait[eng].get(pe, -1) >= val:
                continue
            self.lastwait[eng][pe] = val
            ws.append((pe, val))
        self.ops[eng].append((ws, fn, inc, sem))
        self.cnt[sem] += inc
        return (sem, self.cnt[sem])

    def do(self, eng, fn, reads=(), writes=(), extra=(), dma=False, defer=False):
        waits = list(extra)
        for n in reads:
            waits.append(self.w.get(n))
        for n in writes:
            waits.append(self.w.get(n))
            waits.extend(self.r.get(n, ()))
        if dma:
            dsem = DLSEM[eng]
            self._op(eng, dsem, fn, waits, 16)
            if defer:
                pd = self.pending_dma.setdefault(eng, [])
                pd.extend(("w", n) for n in writes)
                pd.extend(("r", n) for n in reads)
                return None
            pend = self.pending_dma.pop(eng, [])
            tok = self._op(eng, eng, None, [(dsem, self.cnt[dsem])], 1)
            for kind, n in pend:
                if kind == "w":
                    self.w[n] = tok
                    self.r[n] = []
                else:
                    self.r.setdefault(n, []).append(tok)
        else:
            tok = self._op(eng, eng, fn, waits, 1)
        for n in reads:
            self.r.setdefault(n, []).append(tok)
        for n in writes:
            self.w[n] = tok
            self.r[n] = []
        return tok

    def dma_flush(self, eng):
        pend = self.pending_dma.pop(eng, None)
        if not pend:
            return None
        dsem = DLSEM[eng]
        tok = self._op(eng, eng, None, [(dsem, self.cnt[dsem])], 1)
        for kind, n in pend:
            if kind == "w":
                self.w[n] = tok
                self.r[n] = []
            else:
                self.r.setdefault(n, []).append(tok)
        return tok

    def token(self, sem):
        return (sem, self.cnt[sem])


def build_program(debug=False, stage=99):
    nc = bass.Bass("TRN2", target_bir_lowering=False, detect_race_conditions=False)

    dx = nc.dram_tensor("x", [T, D], F32, kind="ExternalInput").ap()
    d_win = nc.dram_tensor("w_in", [D, DIN], F32, kind="ExternalInput").ap()
    d_kx = nc.dram_tensor("kx", [128, 32], F32, kind="ExternalInput").ap()
    d_kz = nc.dram_tensor("kz", [128, 32], F32, kind="ExternalInput").ap()
    d_wxp = nc.dram_tensor("w_xproj", [D, 96], F32, kind="ExternalInput").ap()
    d_wdt = nc.dram_tensor("w_dt", [DTR, D], F32, kind="ExternalInput").ap()
    d_bdt = nc.dram_tensor("b_dt", [128, KC], F32, kind="ExternalInput").ap()
    d_alog = nc.dram_tensor("a_log", [128, 128], F32, kind="ExternalInput").ap()
    d_dd = nc.dram_tensor("d_d", [128, KC], F32, kind="ExternalInput").ap()
    d_wout = nc.dram_tensor("w_out", [DIN, D], F32, kind="ExternalInput").ap()
    d_bout = nc.dram_tensor("b_out", [1, D], F32, kind="ExternalInput").ap()
    d_rw = nc.dram_tensor("r_wT", [D, NE], F32, kind="ExternalInput").ap()
    d_we = nc.dram_tensor("w_e", [NE, D, D], F32, kind="ExternalInput").ap()
    d_c = nc.dram_tensor("c_row", [128, KC], F32, kind="ExternalInput").ap()
    d_wmod = nc.dram_tensor("wmod", [D, 6 * D], F32, kind="ExternalInput").ap()
    d_out = nc.dram_tensor("out", [T, D], F32, kind="ExternalOutput").ap()
    d_x2s = nc.dram_tensor("x2s", [T, D], F32, kind="Internal").ap()
    d_h2s = nc.dram_tensor("h2s", [T, D], F32R, kind="Internal").ap()
    d_tok = nc.dram_tensor("dtok", [NH * 6400, 1], I32, kind="Internal").ap()
    d_ctr = nc.dram_tensor("dctr", [NH * 6400, D], F32, kind="Internal").ap()
    d_werb = nc.dram_tensor("werb", [NE, D, D], BF16, kind="Internal").ap()
    d_winr = nc.dram_tensor("winr", [D, DIN], F32R, kind="Internal").ap()
    d_woutr = nc.dram_tensor("woutr", [DIN, D], F32R, kind="Internal").ap()
    eye = nc.inline_tensor(np.eye(128, dtype=np.float32), "eye").ap()
    toki_c = nc.inline_tensor(
        np.ascontiguousarray(np.arange(T, dtype=np.int32).reshape(NTT, 128).T
                             ).view(np.float32), "toki_c").ap()
    erow_c = nc.inline_tensor(
        (np.arange(NE, dtype=np.float32) * CAP
         + np.array([[0.0], [6400.0]], np.float32)), "erow_c").ap()

    dbg = {}
    if debug:
        for nm, shp in (("xh", [KC, 128, T]), ("z", [KC, 128, T]),
                        ("delta", [KC, 128, T]), ("y", [KC, 128, T]),
                        ("wm", [128, NTT, NE]), ("h2", [KC, 128, T]),
                        ("x2", [T, D]), ("hfm", [KC, 128, 768]),
                        ("mod", [128, 6 * D])):
            dbg[nm] = nc.dram_tensor("dbg_" + nm, shp, F32, kind="ExternalOutput").ap()

    p = Plan()

    def ga(name, shape, dt):
        return nc.alloc_sbuf_tensor(name, shape, dt)

    ones = ga("ones", [128, 128], F32R)
    identr = ga("identr", [128, 128], F32R)
    identf = ga("identf", [128, 128], F32)
    toki = ga("toki", [128, NTT], I32)
    erows = ga("erows", [128, 2, NE], F32)
    modB = ga("modB", [128, 3 * D], F32)
    rwt = ga("rwt", [128, KC, NE], F32R)
    cols = ga("cols", [128, 256], F32)
    stl = ga("stl", [128, 256], F32)

    def settle(eng, resources):
        if eng == "vector":
            p.do(eng, lambda e: e.tensor_copy(out=stl[:, 128:256], in_=stl[:, 0:128]),
                 writes=list(resources))
        else:
            p.do(eng, lambda e: e.copy(stl[:, 128:256], stl[:, 0:128]),
                 writes=list(resources))
    xtile = ga("xtile", [128, 2, D], F32)
    escr = ga("escr", [128, 2, D], F32)
    htm2 = ga("htm2", [128, 2, D], F32R)
    sqscr = ga("sqscr", [128, D], F32)
    recipc = ga("recipc", [128, NTT], F32)
    wmsk = ga("wmsk", [128, NTT, NE], F32)

    stE = contextlib.ExitStack()

    def ea(name, shape, dt):
        return stE.enter_context(nc.sbuf_tensor(name, shape, dt))

    modA = ea("modA", [128, 3 * D], F32)
    gab = ea("gab", [128, D], F32)
    kxc = ea("kxc", [128, 32], F32)
    kzc = ea("kzc", [128, 32], F32)
    aneg = ea("aneg", [128, 128], F32)
    bdt2 = ea("bdt2", [128, KC], F32)
    ddc = ea("ddc", [128, KC], F32)
    wxp = ea("wxp", [128, KC, 96], F32R)
    wdt = ea("wdt", [DTR, D], F32R)
    hstash = ea("hstash", [128, 512], F32)

    pAB = [nc.alloc_psum_tensor(nm, [128, 1024], F32) for nm in ("pA", "pB", "pC", "pD")]
    pA, pB, pC, pD = pAB
    tr0 = pD[:, 0:128].bitcast(F32R)
    tr1 = pD[:, 512:640].bitcast(F32R)

    def bankres(i):
        return ("ps%d_0" % i, "ps%d_1" % i)

    st0 = contextlib.ExitStack()
    silu_c = st0.enter_context(nc.sbuf_tensor("silu_c", [128, KC], F32))
    modw = st0.enter_context(nc.sbuf_tensor("modw", [128, KC, 128], F32R))
    wmbuf = st0.enter_context(nc.sbuf_tensor("wmbuf", [128, 4, 512], F32R))
    p.do("gpsimd", lambda e: e.dma_start(out=identr[:], in_=eye),
         writes=["identr"], dma=True)
    p.do("sync", lambda e: e.dma_start(out=identf[:], in_=eye),
         writes=["identf"], dma=True)
    p.do("sync", lambda e: e.dma_start(out=toki[:].bitcast(F32), in_=toki_c),
         writes=["toki"], dma=True)
    for hh in range(NH):
        p.do("sync", lambda e, hh=hh: e.dma_start(
            out=erows[:, hh, :],
            in_=erow_c[hh:hh + 1].unsqueeze(1).broadcast_to([1, 128, NE])),
            writes=["erows"], dma=True)
    p.do("gpsimd", lambda e: e.memset(ones[:].bitcast(F32), 1.0), writes=["ones"])
    p.do("sync", lambda e: e.dma_start(out=kxc[:], in_=d_kx), writes=["kxc"], dma=True)
    p.do("sync", lambda e: e.dma_start(out=kzc[:], in_=d_kz), writes=["kzc"], dma=True)
    p.do("sync", lambda e: e.dma_start(out=aneg[:], in_=d_alog), writes=["aneg"], dma=True)
    p.do("sync", lambda e: e.dma_start(out=bdt2[:], in_=d_bdt), writes=["bdt2"], dma=True)
    p.do("sync", lambda e: e.dma_start(out=ddc[:], in_=d_dd), writes=["ddc"], dma=True)
    p.do("sync", lambda e: e.dma_start(out=silu_c[:], in_=d_c),
        writes=["silu_c"], dma=True)
    p.do("sync", lambda e: e.dma_start(
        out=gab[:], in_=d_bout.unsqueeze(1).broadcast_to([1, 128, D])),
        writes=["gab"], dma=True)
    p.do("gpsimd", lambda e: e.dma_start(
        out=wxp[:], in_=d_wxp.rearrange("(c q) n -> q c n", q=128)),
        writes=["wxp"], dma=True)
    p.do("gpsimd", lambda e: e.dma_start(out=wdt[:], in_=d_wdt), writes=["wdt"], dma=True)
    p.do("gpsimd", lambda e: e.dma_start(
        out=rwt[:], in_=d_rw.rearrange("(c q) n -> q c n", q=128)),
        writes=["rwt"], dma=True)
    p.do("vector", lambda e: e.memset(stl[:], 0.0), writes=["stl"])
    p.do("vector", lambda e: e.memset(cols[:, 0:1], 1e-5), writes=["c_eps"])
    p.do("vector", lambda e: e.memset(cols[:, 1:2], 1.0), writes=["c_one"])

    p.do("scalar", lambda e: e.activation(out=aneg[:], in_=aneg[:], func=AF.Exp),
         reads=["aneg"], writes=["aneg"])
    p.do("vector", lambda e: e.tensor_scalar(out=aneg[:], in0=aneg[:], scalar1=-1.0,
                                             scalar2=None, op0=OP.mult),
         reads=["aneg"], writes=["aneg"])
    p.do("vector", lambda e: e.tensor_scalar(out=bdt2[:], in0=bdt2[:], scalar1=2.0,
                                             scalar2=None, op0=OP.mult),
         reads=["bdt2"], writes=["bdt2"])
    p.do("scalar", lambda e: e.activation(out=silu_c[:], in_=silu_c[:], func=AF.Silu),
         reads=["silu_c"], writes=["silu_c"])
    settle("scalar", ["silu_c"])
    for c in range(KC):
        p.do("vector", lambda e, c=c: e.tensor_scalar(
            out=modw[:, c, :], in0=ones[:].bitcast(F32), scalar1=silu_c[:, c:c + 1],
            scalar2=None, op0=OP.mult), reads=["ones", "silu_c"], writes=["modw"])

    for nb in range(12):
        pi = nb % 2
        pv = pAB[pi][:, 0:512]
        b0, b1 = bankres(pi)
        for c in range(0, KC, 2):
            for cc in (c, c + 1):
                par = (nb * KC + cc) % 4
                p.do("gpsimd", lambda e, cc=cc, nb=nb, par=par: e.dma_start(
                    out=wmbuf[:, par, :],
                    in_=d_wmod[cc * 128:(cc + 1) * 128, nb * 512:(nb + 1) * 512]),
                    writes=["wmbuf%d" % par], dma=True, defer=True)
            p.dma_flush("gpsimd")
            for cc in (c, c + 1):
                par = (nb * KC + cc) % 4
                pre = (list(p.r.get(b0, ())) + [p.w.get(b0)]) if cc == 0 else []
                p.do("tensor", lambda e, cc=cc, par=par, pv=pv: e.matmul(
                    pv, modw[:, cc, :], wmbuf[:, par, :],
                    start=(cc == 0), stop=(cc == KC - 1)),
                    reads=["modw", "wmbuf%d" % par], extra=pre,
                    writes=([b0] if cc == KC - 1 else []))
        mt = modA if nb < 6 else modB
        mr = "modA" if nb < 6 else "modB"
        mo = (nb % 6) * 512
        p.do("vector", lambda e, mt=mt, mo=mo, pv=pv: e.tensor_copy(
            out=mt[:, mo:mo + 512], in_=pv),
            reads=[b0], writes=[mr])

    def modsl(i):
        t = modA if i < 3 else modB
        return t[:, (i % 3) * D:(i % 3 + 1) * D]

    def modres(i):
        return "modA" if i < 3 else "modB"

    for i in (1, 4):
        p.do("vector", lambda e, i=i: e.tensor_scalar(
            out=modsl(i), in0=modsl(i), scalar1=1.0, scalar2=None, op0=OP.add),
            reads=[modres(i)], writes=[modres(i)])
    for gi in (2, 5):
        p.do("vector", lambda e, gi=gi: e.tensor_scalar(
            out=sqscr[:], in0=modsl(gi), scalar1=-1.0, scalar2=None, op0=OP.mult),
            reads=[modres(gi)], writes=["sqscr"])
        p.do("vector", lambda e, gi=gi: e.tensor_tensor(
            out=sqscr[:], in0=sqscr[:], in1=modsl(gi), op=OP.max),
            reads=["sqscr", modres(gi)], writes=["sqscr"])
        p.do("vector", lambda e, gi=gi: e.tensor_scalar(
            out=sqscr[:], in0=sqscr[:], scalar1=0.25, scalar2=None, op0=OP.max),
            reads=["sqscr"], writes=["sqscr"])
        p.do("vector", lambda e, gi=gi: e.tensor_scalar(
            out=escr[:, 0, :], in0=modsl(gi), scalar1=0.0, scalar2=None, op0=OP.is_ge),
            reads=[modres(gi)], writes=["escr0"])
        p.do("vector", lambda e, gi=gi: e.tensor_scalar(
            out=escr[:, 0, :], in0=escr[:, 0, :], scalar1=2.0, scalar2=-1.0,
            op0=OP.mult, op1=OP.add), reads=["escr0"], writes=["escr0"])
        p.do("vector", lambda e, gi=gi: e.tensor_tensor(
            out=modsl(gi), in0=escr[:, 0, :], in1=sqscr[:], op=OP.mult),
            reads=["escr0", "sqscr"], writes=[modres(gi)])
    p.do("vector", lambda e: e.tensor_tensor(out=gab[:], in0=gab[:], in1=modsl(2),
                                             op=OP.mult),
         reads=["gab", "modA"], writes=["gab"])
    if debug:
        p.do("sync", lambda e: e.dma_start(out=dbg["mod"][:, 0:3 * D], in_=modA[:]),
             reads=["modA"], dma=True)
        p.do("sync", lambda e: e.dma_start(out=dbg["mod"][:, 3 * D:6 * D], in_=modB[:]),
             reads=["modB"], dma=True)

    p.do("gpsimd", lambda e: e.dma_start(out=d_winr, in_=d_win), writes=["winr"],
         dma=True, defer=True)
    p.do("gpsimd", lambda e: e.dma_start(out=d_woutr, in_=d_wout), writes=["woutr"],
         dma=True, defer=True)
    p.dma_flush("gpsimd")
    p.do("scalar", lambda e: e.dma_start(
        out=d_tok[:, 0].rearrange("(b a) -> b a", b=128),
        in_=stl[:, 0:100].bitcast(I32)),
        reads=["stl"], writes=["dtok0", "dtok1"], dma=True)
    st0.close()
    p.fence()
    for ei in range(NE):
        p.do("gpsimd", lambda e, ei=ei: e.dma_start(
            out=d_werb[ei], in_=d_we[ei]),
            writes=["werb%d" % ei], dma=True, defer=True)
    if stage < 1:
        return _emit(nc, p, None)

    def ln_batch(tiles, src_dram, sh_sl, sc_sl, hdst_fn_of, hdst_res, mod_res,
                 interleave=None, h2s=False):
        def cb(j):
            return 16 + 8 * j

        for j0 in range(0, len(tiles), 2):
            pair = [(j, tiles[j]) for j in range(j0, min(j0 + 2, len(tiles)))]
            for j, tt in pair:
                par = tt % 2
                p.do("sync", lambda e, tt=tt, par=par: e.dma_start(
                    out=xtile[:, par, :], in_=src_dram[tt * 128:(tt + 1) * 128, :]),
                    writes=["xtile%d" % par], dma=True, defer=True)
            p.dma_flush("sync")
            for j, tt in pair:
                par = tt % 2
                p.do("vector", lambda e, j=j, par=par: e.tensor_reduce(
                    out=cols[:, cb(j):cb(j) + 1], in_=xtile[:, par, :], axis=AX.X,
                    op=OP.add, negate=True), reads=["xtile%d" % par],
                    writes=["cL%d" % j])
                p.do("scalar", lambda e, j=j, par=par: e.activation(
                    out=sqscr[:], in_=xtile[:, par, :], func=AF.Square,
                    accum_out=cols[:, cb(j) + 2:cb(j) + 3]),
                    reads=["xtile%d" % par], writes=["sqscr", "cQ%d" % j])
        settle("vector", ["cL%d" % j for j in range(len(tiles))])
        for j in range(len(tiles)):
            p.do("scalar", lambda e, j=j: e.mul(
                cols[:, cb(j) + 1:cb(j) + 2], cols[:, cb(j):cb(j) + 1], 1.0 / D),
                reads=["cL%d" % j], writes=["cM%d" % j])
        for j in range(len(tiles)):
            p.do("scalar", lambda e, j=j: e.activation(
                out=cols[:, cb(j) + 3:cb(j) + 4], in_=cols[:, cb(j) + 1:cb(j) + 2],
                func=AF.Square), reads=["cM%d" % j], writes=["c2%d" % j])
        for j in range(len(tiles)):
            p.do("scalar", lambda e, j=j: e.mul(
                cols[:, cb(j) + 4:cb(j) + 5], cols[:, cb(j) + 3:cb(j) + 4], -1.0),
                reads=["c2%d" % j], writes=["c3%d" % j])
        for j in range(len(tiles)):
            p.do("scalar", lambda e, j=j: e.add(
                cols[:, cb(j) + 4:cb(j) + 5], cols[:, cb(j) + 4:cb(j) + 5],
                cols[:, 0:1]), reads=["c3%d" % j, "c_eps"], writes=["c3%d" % j])
        for j in range(len(tiles)):
            p.do("scalar", lambda e, j=j: e.activation(
                out=cols[:, cb(j) + 5:cb(j) + 6], in_=cols[:, cb(j) + 2:cb(j) + 3],
                func=AF.Ln, scale=1.0 / D, bias=cols[:, cb(j) + 4:cb(j) + 5]),
                reads=["cQ%d" % j, "c3%d" % j], writes=["cS%d" % j])
        for j in range(len(tiles)):
            p.do("scalar", lambda e, j=j: e.activation(
                out=cols[:, cb(j) + 6:cb(j) + 7], in_=cols[:, cb(j) + 5:cb(j) + 6],
                func=AF.Exp, scale=-0.5), reads=["cS%d" % j], writes=["cR%d" % j])
        for j, tt in enumerate(tiles):
            par = tt % 2
            hsl = tt % 2 if h2s else 0
            htm = htm2[:, hsl, :]
            hres = "htm%d" % hsl
            eres = "escr%d" % par
            xres = "xtile%d" % par
            p.do("sync", lambda e, tt=tt, par=par: e.dma_start(
                out=xtile[:, par, :], in_=src_dram[tt * 128:(tt + 1) * 128, :]),
                writes=[xres], dma=True)
            p.do("scalar", lambda e, j=j, par=par: e.add(
                escr[:, par, :], xtile[:, par, :], cols[:, cb(j) + 1:cb(j) + 2]),
                reads=[xres, "cM%d" % j], writes=[eres])
            p.do("vector", lambda e, j=j, par=par: e.scalar_tensor_tensor(
                out=escr[:, par, :], in0=escr[:, par, :],
                scalar=cols[:, cb(j) + 6:cb(j) + 7], in1=sc_sl,
                op0=OP.mult, op1=OP.mult),
                reads=[eres, "cR%d" % j, mod_res], writes=[eres])
            p.do("vector", lambda e, par=par, htm=htm: e.tensor_tensor(
                out=htm[:], in0=escr[:, par, :], in1=sh_sl, op=OP.add),
                reads=[eres, mod_res], writes=[hres])
            if h2s:
                p.do("scalar", lambda e, tt=tt, htm=htm: e.dma_start(
                    out=d_h2s[tt * 128:(tt + 1) * 128, :], in_=htm[:]),
                    reads=[hres], writes=["h2s"], dma=True)
            hdst_fn = hdst_fn_of(tt)
            for c in range(KC):
                trb = "ps3_0" if c % 2 == 0 else "ps3_1"
                trv = tr0 if c % 2 == 0 else tr1
                p.do("tensor", lambda e, c=c, trv=trv, htm=htm: e.transpose(
                    trv, htm[:, c * 128:(c + 1) * 128], identr[:]),
                    reads=[hres, "identr"], writes=[trb])
                dst, s0, w = hdst_fn(c)
                if c % 2 == 0:
                    p.do("vector", lambda e, dst=dst, trv=trv, s0=s0, w=w:
                         e.tensor_copy(out=dst, in_=trv.bitcast(F32)[:, s0:s0 + w]),
                         reads=[trb], writes=[hdst_res])
                else:
                    p.do("scalar", lambda e, dst=dst, trv=trv, s0=s0, w=w: e.copy(
                        dst, trv.bitcast(F32)[:, s0:s0 + w]),
                        reads=[trb], writes=[hdst_res])
            if interleave is not None:
                interleave(j)

    stA = contextlib.ExitStack()

    def qa(name, shape, dt):
        return stA.enter_context(nc.sbuf_tensor(name, shape, dt))

    hfm = qa("hfm", [128, KC, 528], F32R)
    rawb = qa("rawb", [128, 2, 520], F32)
    convs = qa("convs", [128, 2, TQ], F32)
    xh = qa("xh", [128, KC, TQ], F32R)
    zs = qa("zs", [128, KC, TQ], F32R)
    delta = qa("delta", [128, KC, TQ], F32)
    yq = qa("yq", [128, KC, TQ], F32R)
    dAs = qa("dAs", [128, 2, TQ], F32)
    bus = qa("bus", [128, 4, TQ], F32)
    hsc = qa("hsc", [128, 2, TQ], F32)
    bbc = qa("bbc", [128, 2, TQ], F32)
    cbc = qa("cbc", [128, 2, TQ], F32)
    xdbl = qa("xdbl", [128, 4, 96], F32R)
    dtT = qa("dtT", [DTR, TQ], F32R)
    bcT = qa("bcT", [32, TQ], F32R)
    winb = qa("winb", [128, 2, KC, 128], F32R)
    woutb = qa("woutb", [128, 2, D], F32R)

    NQ_RUN = NQ if stage >= 6 else 1
    wevac = rawb.ap().rearrange("p a b -> p (a b)")[:, 0:D]
    for q in range(NQ_RUN):
        qs = q * TQ
        qe = qs + TQ
        lo_tile = max(qs // 128 - 1, 0)
        hi_tile = min(qe // 128 + 1, NTT)
        win0 = qs - 8
        win1 = qe + 8

        def hdst_of(tt, win0=win0, win1=win1):
            a = max(tt * 128, win0)
            b = min(tt * 128 + 128, win1)
            off = a - win0
            s0 = a - tt * 128
            w = b - a

            def hdst(c, off=off, s0=s0, w=w):
                return hfm[:, c, off:off + w], s0, w
            return hdst
        ln_batch(list(range(lo_tile, hi_tile)), dx, modsl(0), modsl(1),
                 hdst_of, "hfm", "modA")

        if debug:
            for c in range(KC):
                p.do("sync", lambda e, c=c: e.dma_start(
                    out=dbg["hfm"][c, :, 8:520],
                    in_=hfm[:, c, 8:520].bitcast(F32)),
                    reads=["hfm"], dma=True)
        if stage < 2:
            stA.close()
            return _emit(nc, p, None)

        r0 = max(qs - 1, 0)
        r1 = min(qe + 2, T)
        L = r1 - r0
        c0 = r0 - win0
        LM = min(L, 512)
        LH = (L - LM + 1) // 2 * 2
        i0 = r0 - qs + 2
        if q == NQ - 1:
            for c in range(KC):
                p.do("vector", lambda e, c=c: e.memset(
                    hfm[:, c, 520:524].bitcast(F32), 0.0), writes=["hfm"])
        for m in range(16):
            par = m % 2
            pv = pAB[par]
            b0, b1 = bankres(par)
            p.do("sync", lambda e, m=m, par=par: e.dma_start(
                out=winb[:, par], in_=d_winr[:, m * 128:(m + 1) * 128].rearrange(
                    "(k q) mm -> q k mm", q=128)),
                reads=["winr"], writes=["winb%d" % par], dma=True)
            def win_mms(e, par=par, pv=pv, c0=c0, LM=LM, LH=LH):
                for k in range(KC):
                    mm = e.matmul(pv[:, 0:LM], winb[:, par, k, :],
                                  hfm[:, k, c0:c0 + LM],
                                  start=(k == 0), stop=(k == KC - 1))
                for k in range(KC):
                    mm = e.matmul(pv[:, 512:512 + LH], winb[:, par, k, :],
                                  hfm[:, k, c0 + LM:c0 + LM + LH],
                                  start=(k == 0), stop=(k == KC - 1))
                return mm
            p.do("tensor", win_mms, reads=["winb%d" % par, "hfm"], writes=[b0, b1])
            rres = "rawb%d" % par
            p.do("vector", lambda e, par=par, pv=pv, i0=i0, LM=LM: e.tensor_copy(
                out=rawb[:, par, i0:i0 + LM], in_=pv[:, 0:LM]),
                reads=[b0], writes=[rres])
            p.do("vector", lambda e, par=par, pv=pv, i0=i0, LM=LM, LH=LH: e.tensor_copy(
                out=rawb[:, par, i0 + LM:i0 + LM + LH], in_=pv[:, 512:512 + LH]),
                reads=[b1], writes=[rres])
            if q == 0:
                p.do("vector", lambda e, par=par: e.memset(rawb[:, par, 1:2], 0.0),
                     writes=[rres])
            if q == NQ - 1:
                p.do("vector", lambda e, par=par: e.memset(rawb[:, par, 514:517], 0.0),
                     writes=[rres])
            kcol = kxc if m < KC else kzc
            kres = "kxc" if m < KC else "kzc"
            ccol = (m % KC) * 4
            cres = "convs%d" % par
            p.do("vector", lambda e, par=par, kcol=kcol, ccol=ccol: e.tensor_scalar(
                out=convs[:, par, :], in0=rawb[:, par, 1:513],
                scalar1=kcol[:, ccol:ccol + 1], scalar2=None, op0=OP.mult),
                reads=[rres, kres], writes=[cres])
            for j in (1, 2, 3):
                p.do("vector", lambda e, par=par, kcol=kcol, ccol=ccol, j=j:
                     e.scalar_tensor_tensor(
                         out=convs[:, par, :], in0=rawb[:, par, 1 + j:513 + j],
                         scalar=kcol[:, ccol + j:ccol + j + 1], in1=convs[:, par, :],
                         op0=OP.mult, op1=OP.add),
                     reads=[rres, kres, cres], writes=[cres])
            dstt = xh[:, m % KC] if m < KC else zs[:, m % KC]
            dres = ("xh%d" % (m % KC)) if m < KC else ("zs%d" % (m % KC))
            p.do("scalar", lambda e, dstt=dstt, par=par: e.activation(
                out=dstt, in_=convs[:, par, :], func=AF.Silu),
                reads=[cres], writes=[dres])

        if debug:
            for c in range(KC):
                p.do("sync", lambda e, c=c, qs=qs, qe=qe: e.dma_start(
                    out=dbg["xh"][c, :, qs:qe], in_=xh[:, c].bitcast(F32)),
                    reads=["xh%d" % c], dma=True)
                p.do("sync", lambda e, c=c, qs=qs, qe=qe: e.dma_start(
                    out=dbg["z"][c, :, qs:qe], in_=zs[:, c].bitcast(F32)),
                    reads=["zs%d" % c], dma=True)
        if stage < 3:
            stA.close()
            return _emit(nc, p, None)
        for i in range(4):
            b0, b1 = bankres(2)

            def xp_mms(e, i=i):
                for c in range(KC):
                    mm = e.matmul(pC[:, 0:96], xh[:, c, i * 128:(i + 1) * 128],
                                  wxp[:, c, :], start=(c == 0), stop=(c == KC - 1))
                return mm
            p.do("tensor", xp_mms,
                 reads=["wxp"] + ["xh%d" % c for c in range(KC)], writes=[b0])
            p.do("vector", lambda e, i=i: e.tensor_copy(
                out=xdbl[:, i, :], in_=pC[:, 0:96]),
                reads=[b0], writes=["xdbl%d" % i])
        for i in range(4):
            p.do("tensor", lambda e, i=i: e.transpose(tr0[0:64, :], xdbl[:, i, 0:64], identr[:]),
                 reads=["xdbl%d" % i, "identr"], writes=["ps3_0"])
            p.do("vector", lambda e, i=i: e.tensor_copy(
                out=dtT[:, i * 128:(i + 1) * 128],
                in_=tr0.bitcast(F32)[0:64, :]), reads=["ps3_0"], writes=["dtT"])
            p.do("tensor", lambda e, i=i: e.transpose(tr1[0:32, :], xdbl[:, i, 64:96], identr[:]),
                 reads=["xdbl%d" % i, "identr"], writes=["ps3_1"])
            p.do("vector", lambda e, i=i: e.tensor_copy(
                out=bcT[:, i * 128:(i + 1) * 128],
                in_=tr1.bitcast(F32)[0:32, :]), reads=["ps3_1"], writes=["bcT"])

        for c in range(KC):
            half = c % 2
            pv = pC[:, 0:512] if half == 0 else pC[:, 512:1024]
            bn = bankres(2)[half]
            p.do("tensor", lambda e, c=c, pv=pv: e.matmul(
                pv, wdt[:, c * 128:(c + 1) * 128], dtT[:], start=True, stop=True),
                reads=["dtT", "wdt"], writes=[bn])
            p.do("scalar", lambda e, c=c, pv=pv: e.activation(
                out=convs[:, 0, :], in_=pv, func=AF.Exp, bias=bdt2[:, c:c + 1]),
                reads=[bn, "bdt2", "convs0"], writes=["convs0"])
            p.do("scalar", lambda e, c=c: e.activation(
                out=delta[:, c], in_=convs[:, 0, :], func=AF.Ln, bias=cols[:, 1:2]),
                reads=["convs0", "c_one"], writes=["delta%d" % c])
            p.do("vector", lambda e, c=c: e.tensor_scalar(
                out=yq[:, c], in0=xh[:, c].bitcast(F32),
                scalar1=ddc[:, c:c + 1], scalar2=None, op0=OP.mult),
                reads=["xh%d" % c, "ddc"], writes=["yq%d" % c])
            p.do("vector", lambda e, c=c: e.tensor_tensor(
                out=xh[:, c], in0=delta[:, c], in1=xh[:, c].bitcast(F32),
                op=OP.mult), reads=["delta%d" % c, "xh%d" % c], writes=["xh%d" % c])

        if debug:
            for c in range(KC):
                p.do("sync", lambda e, c=c, qs=qs, qe=qe: e.dma_start(
                    out=dbg["delta"][c, :, qs:qe], in_=delta[:, c]),
                    reads=["delta%d" % c], dma=True)
        if stage < 4:
            stA.close()
            return _emit(nc, p, None)
        def bc_dma(n):
            bp = n % 2
            p.do("sync", lambda e, n=n, bp=bp: e.dma_start(
                out=bbc[:, bp, :],
                in_=bcT.ap().bitcast(F32)[n:n + 1, :].unsqueeze(1)
                    .broadcast_to([1, 128, TQ])),
                reads=["bcT"], writes=["bbc%d" % bp], dma=True, defer=True)
            p.do("sync", lambda e, n=n, bp=bp: e.dma_start(
                out=cbc[:, bp, :],
                in_=bcT.ap().bitcast(F32)[16 + n:17 + n, :].unsqueeze(1)
                    .broadcast_to([1, 128, TQ])),
                reads=["bcT"], writes=["cbc%d" % bp], dma=True, defer=True)
            p.dma_flush("sync")
        bc_dma(0)
        for n in range(NST):
            if n + 1 < NST:
                bc_dma(n + 1)
            for c in range(0, KC, 2):
                i = n * KC + c
                pk = (i // 2) % 3
                if pk == 2:
                    buap = convs[:, 0:2, :]
                    bur = ["convs0", "convs1"]
                else:
                    buap = bus[:, 2 * pk:2 * pk + 2, :]
                    bur = ["bus%d" % (2 * pk), "bus%d" % (2 * pk + 1)]
                for dc in (0, 1):
                    cn = (c + dc) * NST + n
                    sp = (i + dc) % 2
                    p.do("scalar", lambda e, c=c, dc=dc, cn=cn, sp=sp: e.activation(
                        out=dAs[:, sp, :], in_=delta[:, c + dc], func=AF.Exp,
                        scale=aneg[:, cn:cn + 1]),
                        reads=["delta%d" % (c + dc), "aneg"], writes=["dAs%d" % sp])
                p.do("gpsimd", lambda e, c=c, buap=buap, bp=n % 2: e.tensor_tensor(
                    out=buap, in0=xh[:, c:c + 2].bitcast(F32),
                    in1=bbc[:, bp, :].unsqueeze(1).broadcast_to([128, 2, TQ]),
                    op=OP.mult),
                    reads=["xh%d" % c, "xh%d" % (c + 1), "bbc%d" % (n % 2)],
                    writes=bur)
                for dc in (0, 1):
                    cn = (c + dc) * NST + n
                    sp = (i + dc) % 2
                    init = 0.0 if q == 0 else hstash[:, cn * 4 + 3:cn * 4 + 4]
                    rds = ["dAs%d" % sp, bur[dc]]
                    if q > 0:
                        rds.append("hst%d" % cn)
                    p.do("vector", lambda e, sp=sp, buap=buap, dc=dc, init=init:
                         e.tensor_tensor_scan(
                             out=hsc[:, sp, :], data0=dAs[:, sp, :],
                             data1=buap[:, dc, :],
                             initial=init, op0=OP.mult, op1=OP.add),
                         reads=rds, writes=["hsc%d" % sp])
                    p.do("scalar", lambda e, sp=sp, cn=cn: e.copy(
                        hstash[:, cn * 4:cn * 4 + 4], hsc[:, sp, TQ - 4:TQ]),
                        reads=["hsc%d" % sp], writes=["hst%d" % cn])
                p.do("vector", lambda e, buap=buap, bp=n % 2: e.tensor_tensor(
                    out=buap, in0=hsc[:, :, :],
                    in1=cbc[:, bp, :].unsqueeze(1).broadcast_to([128, 2, TQ]),
                    op=OP.mult),
                    reads=["hsc0", "hsc1", "cbc%d" % (n % 2)],
                    writes=bur)
                p.do("vector", lambda e, c=c, buap=buap: e.tensor_tensor(
                    out=yq[:, c:c + 2], in0=yq[:, c:c + 2].bitcast(F32),
                    in1=buap, op=OP.add),
                    reads=["yq%d" % c, "yq%d" % (c + 1)] + bur,
                    writes=["yq%d" % c, "yq%d" % (c + 1)])

        if debug:
            for c in range(KC):
                p.do("sync", lambda e, c=c, qs=qs, qe=qe: e.dma_start(
                    out=dbg["y"][c, :, qs:qe], in_=yq[:, c].bitcast(F32)),
                    reads=["yq%d" % c], dma=True)

        if stage < 5:
            stA.close()
            return _emit(nc, p, None)
        for k in range(16):
            par = k % 2
            src = yq if k < KC else zs
            sres = ("yq%d" % (k % KC)) if k < KC else ("zs%d" % (k % KC))
            cc = k % KC
            p.do("scalar", lambda e, k=k, par=par: e.dma_start(
                out=woutb[:, par, :], in_=d_woutr[k * 128:(k + 1) * 128, :]),
                reads=["woutr"], writes=["woutb%d" % par], dma=True)
            def wo_mms(e, k=k, par=par, src=src, cc=cc):
                for i in range(4):
                    for nn in range(2):
                        mm = e.matmul(pAB[i][:, nn * 512:(nn + 1) * 512],
                                      src[:, cc, i * 128:(i + 1) * 128],
                                      woutb[:, par, nn * 512:(nn + 1) * 512],
                                      start=(k == 0), stop=(k == 15))
                return mm
            allb = [bn for i in range(4) for bn in bankres(i)]
            p.do("tensor", wo_mms, reads=["woutb%d" % par, sres],
                 writes=(allb if k == 0 else allb))
        for i in range(4):
            tt = qs // 128 + i
            par = i % 2
            b0, b1 = bankres(i)
            xres = "xtile%d" % par
            p.do("sync", lambda e, tt=tt, par=par: e.dma_start(
                out=xtile[:, par, :], in_=dx[tt * 128:(tt + 1) * 128, :]),
                writes=[xres], dma=True)
            p.do("vector", lambda e, i=i: e.tensor_tensor(
                out=wevac, in0=pAB[i][:], in1=modsl(2), op=OP.mult),
                reads=[b0, b1, "modA", "rawb0", "rawb1"],
                writes=["rawb0", "rawb1"])
            p.do("vector", lambda e: e.tensor_tensor(
                out=wevac, in0=wevac, in1=gab[:], op=OP.add),
                reads=["rawb0", "rawb1", "gab"], writes=["rawb0", "rawb1"])
            p.do("vector", lambda e, par=par: e.tensor_tensor(
                out=xtile[:, par, :], in0=xtile[:, par, :], in1=wevac,
                op=OP.add), reads=[xres, "rawb0", "rawb1"], writes=[xres])
            p.do("sync", lambda e, tt=tt, par=par: e.dma_start(
                out=d_x2s[tt * 128:(tt + 1) * 128, :], in_=xtile[:, par, :]),
                reads=[xres], writes=["x2s"], dma=True)
            if debug:
                p.do("sync", lambda e, tt=tt, par=par: e.dma_start(
                    out=dbg["x2"][tt * 128:(tt + 1) * 128, :], in_=xtile[:, par, :]),
                    reads=[xres], dma=True)

    stA.close()
    stE.close()
    p.fence()
    if stage < 7:
        return _emit(nc, p, None)

    stB = contextlib.ExitStack()

    def ba(name, shape, dt):
        return stB.enter_context(nc.sbuf_tensor(name, shape, dt))

    h2fm = ba("h2fm", [128, KC, TH], F32R)
    web2 = ba("web2", [128, 2, KC, D], BF16)
    lhsT = ba("lhsT", [128, 2, KC, CAP], BF16)
    gbA = ba("gbA", [128, 2, D], F32R)
    gbB = ba("gbB", [64, 2, D], F32R)
    gct = ba("gct", [128, 2, 2, D], F32)
    wmT = ba("wmT", [NE, TH], F32)
    ztE = ba("ztE", [NE, TH], F32)
    ppf = ba("ppf", [NE, TH], F32)
    ptokb = ba("ptokb", [128, KC, NE], F32)
    growb = ba("growb", [128, KC, NE], F32)
    selb = ba("selb", [128, KC, NE], F32)
    scrb = ba("scrb", [128, KC, NE], F32)
    colsb = ba("colsb", [128, KC, 8], F32)
    cidx = ba("cidx", [128, NTT, 2], I32)
    wab = ba("wab", [128, NTT, 2], F32)
    gi32 = ba("gi32", [128, 2, 2], I32)
    ctrb = ba("ctrb", [128, 2, D], F32)
    ctrb1 = ba("ctrb1", [64, 2, D], F32)
    p.dma_flush("gpsimd")
    p.do("vector", lambda e: e.memset(ztE[:], 0.0), writes=["ztE"])

    for pe_ in (0, 1):
        p.do("sync" if pe_ == 0 else "scalar", lambda e, ei=pe_, par=pe_:
             e.dma_start(out=web2[:, par], in_=d_werb[ei:ei + 1].rearrange(
                 "o (c q) n -> q (o c) n", q=128)),
             reads=["werb%d" % pe_], writes=["web%d" % pe_], dma=True)

    def emit_ln_router(hh):
        hs0 = hh * TH

        def h2dst_of(tt, hs0=hs0):
            off = (tt - hs0 // 128) * 128

            def hdst(c, off=off):
                return h2fm[:, c, off:off + 128], 0, 128
            return hdst
        tiles = [hs0 // 128 + i for i in range(8)]
        ln_batch(tiles, d_x2s, modsl(3), modsl(4), h2dst_of, "h2fm", "modB",
                 h2s=True)

        RB = [("ps2_0", pC[:, 0:NE]), ("ps2_1", pC[:, 512:512 + NE]),
              ("ps3_0", pD[:, 0:NE]), ("ps3_1", pD[:, 512:512 + NE])]
        for g in range(2):
            idxs = list(range(g * 4, (g + 1) * 4))
            for j, i in enumerate(idxs):
                off = i * 128
                bn, pv = RB[j]

                def rt_mms(e, off=off, pv=pv):
                    for c in range(KC):
                        mm = e.matmul(pv, h2fm[:, c, off:off + 128],
                                      rwt[:, c, :], start=(c == 0),
                                      stop=(c == KC - 1))
                    return mm
                p.do("tensor", rt_mms, reads=["h2fm", "rwt"], writes=[bn])
            for j, i in enumerate(idxs):
                bn, pv = RB[j]
                p.do("vector", lambda e, j=j, pv=pv: e.tensor_reduce(
                    out=cols[:, 80 + j:81 + j], in_=pv, axis=AX.X, op=OP.max,
                    negate=True), reads=[bn], writes=["cX%d" % j])
            settle("vector", ["cX%d" % j for j in range(4)])
            for j, i in enumerate(idxs):
                tt = hs0 // 128 + i
                bn, pv = RB[j]
                p.do("scalar", lambda e, j=j, tt=tt, pv=pv: e.activation(
                    out=wmsk[:, tt, :], in_=pv, func=AF.Exp,
                    bias=cols[:, 80 + j:81 + j], accum_out=cols[:, 84 + j:85 + j]),
                    reads=[bn, "cX%d" % j], writes=["wm%d" % tt, "cY%d" % j])
            settle("scalar", ["cY%d" % j for j in range(4)] +
                   ["wm%d" % (hs0 // 128 + i) for i in idxs])
            for j, i in enumerate(idxs):
                p.do("scalar", lambda e, j=j: e.activation(
                    out=cols[:, 88 + j:89 + j], in_=cols[:, 84 + j:85 + j],
                    func=AF.Ln), reads=["cY%d" % j], writes=["cZ%d" % j])
            for j, i in enumerate(idxs):
                tt = hs0 // 128 + i
                p.do("scalar", lambda e, j=j, tt=tt: e.activation(
                    out=recipc[:, tt:tt + 1], in_=cols[:, 88 + j:89 + j],
                    func=AF.Exp, scale=-1.0), reads=["cZ%d" % j], writes=["recip"])
            for j, i in enumerate(idxs):
                tt = hs0 // 128 + i
                p.do("vector", lambda e, j=j, tt=tt: e.max(
                    out=cols[:, 92 + 8 * j:100 + 8 * j], in_=wmsk[:, tt, :]),
                    reads=["wm%d" % tt], writes=["c8%d" % j])
            settle("vector", ["c8%d" % j for j in range(4)])
            for j, i in enumerate(idxs):
                tt = hs0 // 128 + i
                p.do("vector", lambda e, j=j, tt=tt: e.tensor_scalar(
                    out=cols[:, 128 + 32 * j:160 + 32 * j], in0=wmsk[:, tt, :],
                    scalar1=cols[:, 93 + 8 * j:94 + 8 * j],
                    scalar2=None, op0=OP.is_ge),
                    reads=["wm%d" % tt, "c8%d" % j], writes=["cK%d" % j])
            settle("vector", ["cK%d" % j for j in range(4)])
            for j, i in enumerate(idxs):
                tt = hs0 // 128 + i
                p.do("vector", lambda e, j=j, tt=tt: e.tensor_tensor(
                    out=wmsk[:, tt, :], in0=wmsk[:, tt, :],
                    in1=cols[:, 128 + 32 * j:160 + 32 * j],
                    op=OP.mult), reads=["wm%d" % tt, "cK%d" % j], writes=["wm%d" % tt])
        if debug:
            for i in range(8):
                tt = hs0 // 128 + i
                off = i * 128
                p.do("sync", lambda e, tt=tt: e.dma_start(
                    out=dbg["wm"][:, tt, :], in_=wmsk[:, tt, :]),
                    reads=["wm%d" % tt], dma=True)
                for c in range(KC):
                    p.do("sync", lambda e, c=c, off=off, hs0=hs0: e.dma_start(
                        out=dbg["h2"][c, :, hs0 + off:hs0 + off + 128],
                        in_=h2fm[:, c, off:off + 128].bitcast(F32)),
                        reads=["h2fm"], dma=True)

    def emit_index(hh):
        hb = hh * KC
        for j in range(KC):
            tt = hb + j
            bn = "ps2_0" if j < 4 else "ps2_1"
            pvw = pC[0:NE, j * 128:(j + 1) * 128]
            p.do("tensor", lambda e, tt=tt, pvw=pvw: e.transpose(
                pvw, wmsk[:, tt, :], identf[:]),
                reads=["wm%d" % tt, "identf"], writes=[bn])
        for j in range(KC):
            bn = "ps2_0" if j < 4 else "ps2_1"
            pvw = pC[0:NE, j * 128:(j + 1) * 128]
            eng2 = "vector" if j % 2 == 0 else "scalar"
            if eng2 == "vector":
                p.do(eng2, lambda e, j=j, pvw=pvw: e.tensor_copy(
                    out=wmT[:, j * 128:(j + 1) * 128], in_=pvw),
                    reads=[bn], writes=["wmT"])
            else:
                p.do(eng2, lambda e, j=j, pvw=pvw: e.copy(
                    wmT[:, j * 128:(j + 1) * 128], pvw),
                    reads=[bn], writes=["wmT"])
        p.do("vector", lambda e: e.tensor_scalar(
            out=wmT[:], in0=wmT[:], scalar1=0.0, scalar2=None, op0=OP.is_gt),
            reads=["wmT"], writes=["wmT"])
        p.do("vector", lambda e: e.tensor_tensor_scan(
            out=ppf[:], data0=ztE[:], data1=wmT[:], initial=0.0,
            op0=OP.add, op1=OP.add), reads=["ztE", "wmT"], writes=["ppf"])
        for j in range(KC):
            p.do("tensor", lambda e, j=j: e.transpose(
                pD[0:128, 256 + j * NE:256 + (j + 1) * NE],
                ppf[:, j * 128:(j + 1) * 128], identf[0:NE, 0:NE]),
                reads=["ppf", "identf"], writes=["ps3_0"])
        for j in range(KC):
            eng2 = "vector" if j % 2 == 0 else "scalar"
            pvw = pD[0:128, 256 + j * NE:256 + (j + 1) * NE]
            if eng2 == "vector":
                p.do(eng2, lambda e, j=j, pvw=pvw: e.tensor_copy(
                    out=ptokb[:, j, :], in_=pvw), reads=["ps3_0"], writes=["ixs"])
            else:
                p.do(eng2, lambda e, j=j, pvw=pvw: e.copy(
                    ptokb[:, j, :], pvw), reads=["ps3_0"], writes=["ixs"])
        for j in range(KC):
            p.do("vector", lambda e, j=j: e.tensor_scalar(
                out=ptokb[:, j, :], in0=ptokb[:, j, :], scalar1=-1.0,
                scalar2=float(CAP - 1), op0=OP.add, op1=OP.min),
                reads=["ixs"], writes=["ixs"])
        for j in range(KC):
            p.do("vector", lambda e, j=j, hh=hh: e.tensor_tensor(
                out=growb[:, j, :], in0=ptokb[:, j, :], in1=erows[:, hh, :],
                op=OP.add), reads=["ixs", "erows"], writes=["ixs"])
        for j in range(KC):
            tt = hb + j
            p.do("vector", lambda e, j=j, tt=tt: e.tensor_scalar(
                out=selb[:, j, :], in0=wmsk[:, tt, :], scalar1=0.0,
                scalar2=None, op0=OP.is_gt),
                reads=["wm%d" % tt], writes=["ixs"])
        for j in range(KC):
            p.do("vector", lambda e, j=j: e.tensor_scalar(
                out=scrb[:, j, :], in0=growb[:, j, :], scalar1=-1.0,
                scalar2=BIGC, op0=OP.mult, op1=OP.add),
                reads=["ixs"], writes=["ixs"])
        for j in range(KC):
            p.do("vector", lambda e, j=j: e.tensor_tensor(
                out=scrb[:, j, :], in0=scrb[:, j, :], in1=selb[:, j, :],
                op=OP.mult), reads=["ixs"], writes=["ixs"])
        for j in range(KC):
            p.do("vector", lambda e, j=j: e.tensor_reduce(
                out=colsb[:, j, 0:1], in_=scrb[:, j, :], axis=AX.X,
                op=OP.max), reads=["ixs"], writes=["ixs"])
        for j in range(KC):
            p.do("vector", lambda e, j=j: e.tensor_scalar(
                out=scrb[:, j, :], in0=growb[:, j, :], scalar1=1.0,
                scalar2=None, op0=OP.add), reads=["ixs"], writes=["ixs"])
        for j in range(KC):
            p.do("vector", lambda e, j=j: e.tensor_tensor(
                out=scrb[:, j, :], in0=scrb[:, j, :], in1=selb[:, j, :],
                op=OP.mult), reads=["ixs"], writes=["ixs"])
        for j in range(KC):
            p.do("vector", lambda e, j=j: e.tensor_reduce(
                out=colsb[:, j, 1:2], in_=scrb[:, j, :], axis=AX.X,
                op=OP.max), reads=["ixs"], writes=["ixs"])
        for j in range(KC):
            p.do("vector", lambda e, j=j: e.tensor_scalar(
                out=colsb[:, j, 2:3], in0=colsb[:, j, 0:1], scalar1=-1.0,
                scalar2=BIGC, op0=OP.mult, op1=OP.add),
                reads=["ixs"], writes=["ixs"])
        for j in range(KC):
            p.do("vector", lambda e, j=j: e.tensor_scalar(
                out=colsb[:, j, 3:4], in0=colsb[:, j, 1:2], scalar1=-1.0,
                scalar2=None, op0=OP.add), reads=["ixs"], writes=["ixs"])
        for j in range(KC):
            tt = hb + j
            p.do("vector", lambda e, j=j, tt=tt: e.tensor_copy(
                out=cidx[:, tt, :], in_=colsb[:, j, 2:4]),
                reads=["ixs"], writes=["cidx%d" % tt])
        for j in range(KC):
            p.do("vector", lambda e, j=j: e.tensor_scalar(
                out=scrb[:, j, :], in0=growb[:, j, :],
                scalar1=colsb[:, j, 2:3], scalar2=None, op0=OP.is_equal),
                reads=["ixs"], writes=["ixs"])
        for j in range(KC):
            tt = hb + j
            p.do("vector", lambda e, j=j, tt=tt: e.tensor_tensor(
                out=scrb[:, j, :], in0=scrb[:, j, :], in1=wmsk[:, tt, :],
                op=OP.mult), reads=["ixs", "wm%d" % tt], writes=["ixs"])
        for j in range(KC):
            tt = hb + j
            p.do("vector", lambda e, j=j, tt=tt: e.tensor_reduce(
                out=wab[:, tt, 0:1], in_=scrb[:, j, :], axis=AX.X,
                op=OP.add), reads=["ixs"], writes=["wab%d" % tt])
        for j in range(KC):
            p.do("vector", lambda e, j=j: e.tensor_scalar(
                out=scrb[:, j, :], in0=growb[:, j, :],
                scalar1=colsb[:, j, 3:4], scalar2=None, op0=OP.is_equal),
                reads=["ixs"], writes=["ixs"])
        for j in range(KC):
            tt = hb + j
            p.do("vector", lambda e, j=j, tt=tt: e.tensor_tensor(
                out=scrb[:, j, :], in0=scrb[:, j, :], in1=wmsk[:, tt, :],
                op=OP.mult), reads=["ixs", "wm%d" % tt], writes=["ixs"])
        for j in range(KC):
            tt = hb + j
            p.do("vector", lambda e, j=j, tt=tt: e.tensor_reduce(
                out=wab[:, tt, 1:2], in_=scrb[:, j, :], axis=AX.X,
                op=OP.add), reads=["ixs"], writes=["wab%d" % tt])
        for j in range(KC):
            tt = hb + j
            for k in range(2):
                p.do("gpsimd", lambda e, tt=tt, k=k:
                     e.indirect_dma_start(
                         out=d_tok, out_offset=IOA(cidx[:, tt, k:k + 1], 0),
                         in_=toki[:, tt:tt + 1], in_offset=None),
                     reads=["cidx%d" % tt, "toki", "dtok%d" % hh],
                     dma=True, defer=True)
        for k2 in range(2):
            p.pending_dma.setdefault("gpsimd", []).append(("w", "dtok%d" % hh))
        p.dma_flush("gpsimd")

    def emit_experts(hh):
        base = hh * 6400

        def rbk(ei):
            par = ei % 2
            p.do("gpsimd", lambda e, ei=ei, par=par: e.dma_start(
                out=gi32[:, par, 0:1],
                in_=d_tok[base + ei * CAP:base + ei * CAP + 128, :]),
                reads=["dtok%d" % hh], writes=["gi%d" % par], dma=True,
                defer=True)
            p.do("gpsimd", lambda e, ei=ei, par=par: e.dma_start(
                out=gi32[0:64, par, 1:2],
                in_=d_tok[base + ei * CAP + 128:base + ei * CAP + 192, :]),
                reads=["dtok%d" % hh], writes=["gi%d" % par], dma=True,
                defer=True)

        def gth(ei):
            par = ei % 2
            p.do("gpsimd", lambda e, par=par: e.indirect_dma_start(
                out=gbA[:, par, :], out_offset=None, in_=d_h2s,
                in_offset=IOA(gi32[:, par, 0:1], 0)),
                reads=["gi%d" % par, "h2s"], writes=["gbA%d" % par],
                dma=True, defer=True)
            p.do("gpsimd", lambda e, par=par: e.indirect_dma_start(
                out=gbB[:, par, :], out_offset=None, in_=d_h2s,
                in_offset=IOA(gi32[0:64, par, 1:2], 0)),
                reads=["gi%d" % par, "h2s"], writes=["gbB%d" % par],
                dma=True, defer=True)
        TRS = [(pC[:, 0:CAP], "ps2_0"), (pC[:, 512:512 + CAP], "ps2_1"),
               (pD[:, 0:CAP], "ps3_0"), (pD[:, 512:512 + CAP], "ps3_1")]

        def trans(ei):
            par = ei % 2
            for c in range(KC):
                slot, bn = TRS[c % 4]
                slotr = slot.bitcast(F32R)
                p.do("tensor", lambda e, c=c, par=par, slotr=slotr: e.transpose(
                    slotr[:, 0:128], gbA[:, par, c * 128:(c + 1) * 128],
                    identr[:]),
                    reads=["gbA%d" % par, "identr"], writes=[bn])
                p.do("tensor", lambda e, c=c, par=par, slotr=slotr: e.transpose(
                    slotr[:, 128:192], gbB[0:64, par, c * 128:(c + 1) * 128],
                    identr[0:64, 0:64]),
                    reads=["gbB%d" % par, "identr"], writes=[bn])
                p.do("vector", lambda e, c=c, par=par, slot=slot: e.tensor_copy(
                    out=lhsT[:, par, c, :], in_=slot),
                    reads=[bn], writes=["lhs%d" % par])
        rbk(0)
        p.dma_flush("gpsimd")
        gth(0)
        rbk(1)
        p.dma_flush("gpsimd")
        gth(1)
        rbk(2)
        p.dma_flush("gpsimd")
        trans(0)
        for ei in range(NE):
            par = ei % 2
            qeng = "sync" if par == 0 else "scalar"
            wres = "web%d" % par
            if ei + 2 < NE:
                gth(ei + 2)
            if ei + 3 < NE:
                rbk(ei + 3)
            p.dma_flush("gpsimd")
            if ei + 1 < NE:
                trans(ei + 1)
            pblk = pAB[0]
            bka, bkb = bankres(0)
            b1a, b1b = bankres(1)

            def fe_mms(e, par=par, pblk=pblk):
                for c in range(KC):
                    for nn in range(2):
                        e.matmul(pblk[:, nn * 512:(nn + 1) * 512],
                                 lhsT[:, par, c, 0:128],
                                 web2[:, par, c, nn * 512:(nn + 1) * 512],
                                 start=(c == 0), stop=(c == KC - 1))
                for c in range(KC):
                    for nn in range(2):
                        mm = e.matmul(pB[0:64, nn * 512:(nn + 1) * 512],
                                      lhsT[:, par, c, 128:192],
                                      web2[:, par, c, nn * 512:(nn + 1) * 512],
                                      start=(c == 0), stop=(c == KC - 1))
                return mm
            p.do("tensor", fe_mms, reads=["lhs%d" % par, wres],
                 writes=[bka, bkb, b1a, b1b])
            wei = ei + 2
            if wei < NE or hh == 0:
                wsrc = wei if wei < NE else wei - NE
                p.do(qeng, lambda e, wsrc=wsrc, par=par: e.dma_start(
                    out=web2[:, par],
                    in_=d_werb[wsrc:wsrc + 1].rearrange(
                        "o (c q) n -> q (o c) n", q=128)),
                    reads=["werb%d" % wsrc], writes=[wres], dma=True,
                    defer=True)
            ctres = "ctr%s%d" % ("E" if par == 0 else "O", hh)
            p.do("vector", lambda e, par=par, pblk=pblk: e.tensor_copy(
                out=ctrb[:, par, :], in_=pblk[:]),
                reads=[bka, bkb], writes=["ctrb%d" % par])
            p.do("vector", lambda e, par=par: e.tensor_copy(
                out=ctrb1[:, par, :], in_=pB[0:64, :]),
                reads=[b1a, b1b], writes=["ctrb1%d" % par])
            p.do(qeng, lambda e, ei=ei, par=par: e.dma_start(
                out=d_ctr[base + ei * CAP:base + ei * CAP + 128, :],
                in_=ctrb[:, par, :]),
                reads=["ctrb%d" % par], writes=[ctres], dma=True, defer=True)
            p.do(qeng, lambda e, ei=ei, par=par: e.dma_start(
                out=d_ctr[base + ei * CAP + 128:base + ei * CAP + 192, :],
                in_=ctrb1[:, par, :]),
                reads=["ctrb1%d" % par], writes=[ctres], dma=True, defer=True)
            p.dma_flush(qeng)

    def emit_combine(hh):
        for i in range(8):
            tt = hh * KC + i
            tp = i % 2
            xres = "xtile%d" % tp
            gres = "gc%d" % tp
            for k in range(2):
                p.do("gpsimd", lambda e, tt=tt, tp=tp, k=k:
                     e.indirect_dma_start(
                         out=gct[:, tp, k, :], out_offset=None, in_=d_ctr,
                         in_offset=IOA(cidx[:, tt, k:k + 1], 0)),
                     reads=["cidx%d" % tt, "ctrE%d" % hh, "ctrO%d" % hh],
                     writes=[gres], dma=True, defer=True)
            p.dma_flush("gpsimd")
            p.do("sync", lambda e, tt=tt, tp=tp: e.dma_start(
                out=xtile[:, tp, :], in_=d_x2s[tt * 128:(tt + 1) * 128, :]),
                reads=["x2s"], writes=[xres], dma=True)
            p.do("vector", lambda e, tt=tt, tp=tp: e.tensor_scalar(
                out=escr[:, tp, :], in0=gct[:, tp, 0, :],
                scalar1=wab[:, tt, 0:1], scalar2=None, op0=OP.mult),
                reads=[gres, "wab%d" % tt], writes=["escr%d" % tp])
            p.do("vector", lambda e, tt=tt, tp=tp: e.scalar_tensor_tensor(
                out=escr[:, tp, :], in0=gct[:, tp, 1, :],
                scalar=wab[:, tt, 1:2], in1=escr[:, tp, :],
                op0=OP.mult, op1=OP.add),
                reads=[gres, "wab%d" % tt, "escr%d" % tp], writes=["escr%d" % tp])
            p.do("vector", lambda e, tt=tt, tp=tp: e.scalar_tensor_tensor(
                out=escr[:, tp, :], in0=escr[:, tp, :],
                scalar=recipc[:, tt:tt + 1], in1=modsl(5),
                op0=OP.mult, op1=OP.mult),
                reads=["escr%d" % tp, "recip", "modB"], writes=["escr%d" % tp])
            p.do("vector", lambda e, tp=tp: e.tensor_tensor(
                out=xtile[:, tp, :], in0=xtile[:, tp, :], in1=escr[:, tp, :],
                op=OP.add), reads=[xres, "escr%d" % tp], writes=[xres])
            p.do("sync", lambda e, tt=tt, tp=tp: e.dma_start(
                out=d_out[tt * 128:(tt + 1) * 128, :], in_=xtile[:, tp, :]),
                reads=[xres], writes=["outd"], dma=True)

    emit_ln_router(0)
    emit_index(0)
    if stage < 8:
        stB.close()
        return _emit(nc, p, None)
    emit_ln_router(1)
    emit_index(1)
    emit_experts(0)
    emit_combine(0)
    emit_experts(1)
    emit_combine(1)

    stB.close()
    return _emit(nc, p, p.w["outd"])


def _emit(nc, p, final_out):
    with (
        nc.semaphore("s_sync") as s_sync,
        nc.semaphore("s_gp") as s_gp,
        nc.semaphore("s_sdl") as s_sdl,
        nc.semaphore("s_gdl") as s_gdl,
        nc.semaphore("s_scd") as s_scd,
        nc.semaphore("s_ved") as s_ved,
        nc.semaphore("s_v") as s_v,
        nc.semaphore("s_s") as s_s,
        nc.semaphore("s_t") as s_t,
        nc.Block() as block,
    ):
        sems = {"sync": s_sync, "gpsimd": s_gp, "syncdl": s_sdl, "gpdl": s_gdl,
                "scdl": s_scd, "vedl": s_ved, "vector": s_v, "scalar": s_s,
                "tensor": s_t}

        def runner(name, final_waits=()):
            def _run(eng):
                for ws, fn, inc, sem in p.ops[name]:
                    for pe, val in ws:
                        eng.wait_ge(sems[pe], val)
                    if fn is None:
                        eng.sem_inc(sems[sem], inc)
                    else:
                        inst = fn(eng)
                        inst.then_inc(sems[sem], inc)
                for pe, val in final_waits:
                    if pe != name:
                        eng.wait_ge(sems[pe], val)
            return _run

        fw = ([final_out] if final_out else []) + [(e, p.cnt[e]) for e in SEMS]
        block.sync(runner("sync", final_waits=fw))
        block.gpsimd(runner("gpsimd"))
        block.vector(runner("vector"))
        block.scalar(runner("scalar"))
        block.tensor(runner("tensor"))

    return nc


def stage_inputs(inputs):
    f = np.ascontiguousarray
    x = inputs["L0_attn_x"]
    kx = inputs["L0_attn_conv_x_k"][:, 0, :].T.reshape(KC, 128, 4).transpose(1, 0, 2).reshape(128, 32)
    kz = inputs["L0_attn_conv_z_k"][:, 0, :].T.reshape(KC, 128, 4).transpose(1, 0, 2).reshape(128, 32)
    alog = inputs["L0_attn_A_log"].reshape(KC, 128, NST).transpose(1, 0, 2).reshape(128, 128)
    bdt = inputs["L0_attn_b_dt"].reshape(KC, 128).T
    dd = inputs["L0_attn_D"].reshape(KC, 128).T
    rwT = inputs["L0_ffn_router_w"].T
    common = dict(
        w_in=f(inputs["L0_attn_W_in"]),
        kx=f(kx), kz=f(kz),
        w_xproj=f(inputs["L0_attn_W_xproj"]),
        w_dt=f(inputs["L0_attn_W_dt"]),
        b_dt=f(bdt), a_log=f(alog), d_d=f(dd),
        w_out=f(inputs["L0_attn_W_out"]),
        b_out=f(inputs["L0_attn_b_out"].reshape(1, D)),
        r_wT=f(rwT),
        w_e=f(inputs["L0_ffn_expert_weights"]),
        wmod=f(inputs["L0_wmod"]),
    )
    maps = []
    for b in range(8):
        m = dict(common)
        m["x"] = f(np.asarray(x[b]))
        m["c_row"] = f(np.asarray(inputs["L0_c"][b]).reshape(KC, 128).T)
        maps.append(m)
    return maps


def kernel(**inputs):
    maps = stage_inputs(inputs)
    nc = build_program(debug=False)
    res = run_bass_kernel_spmd(nc, maps, core_ids=list(range(8)))
    return np.stack([np.asarray(r["out"]) for r in res.results], axis=0)


# version 18
# speedup vs baseline: 1.0835x; 1.0172x over previous
import contextlib
import numpy as np
import concourse.bass as bass
import concourse.mybir as mybir
from concourse.bass_utils import run_bass_kernel_spmd

F32 = mybir.dt.float32
F32R = mybir.dt.float32r
BF16 = mybir.dt.bfloat16
I32 = mybir.dt.int32
IOA = bass.IndirectOffsetOnAxis
AF = mybir.ActivationFunctionType
OP = mybir.AluOpType
AX = mybir.AxisListType

T = 2048
D = 1024
DIN = 2048
NST = 16
DTR = 64
NE = 32
NQ = 4
TQ = T // NQ
NH = 2
TH = T // NH
NTT = T // 128
KC = D // 128
CAP = 192
BIGC = 16384.0

ENGS = ("sync", "gpsimd", "vector", "scalar", "tensor")
SEMS = ("sync", "gpsimd", "vector", "scalar", "tensor", "syncdl", "gpdl", "scdl", "vedl")
DLSEM = {"sync": "syncdl", "gpsimd": "gpdl", "scalar": "scdl", "vector": "vedl"}


class Plan:

    def __init__(self):
        self.ops = {e: [] for e in ENGS}
        self.cnt = {e: 0 for e in SEMS}
        self.lastwait = {e: {} for e in ENGS}
        self.w = {}
        self.r = {}
        self.pending_fence = {}
        self.pending_dma = {}

    def fence(self):
        toks = [(e, self.cnt[e]) for e in SEMS if self.cnt[e] > 0]
        for e in ENGS:
            self.pending_fence[e] = list(toks)

    def _op(self, eng, sem, fn, waits, inc):
        if self.pending_fence.get(eng):
            waits = list(waits) + self.pending_fence.pop(eng)
        ws = []
        for tok in waits:
            if tok is None:
                continue
            pe, val = tok
            if pe == eng:
                continue
            if self.lastwait[eng].get(pe, -1) >= val:
                continue
            self.lastwait[eng][pe] = val
            ws.append((pe, val))
        self.ops[eng].append((ws, fn, inc, sem))
        self.cnt[sem] += inc
        return (sem, self.cnt[sem])

    def do(self, eng, fn, reads=(), writes=(), extra=(), dma=False, defer=False):
        waits = list(extra)
        for n in reads:
            waits.append(self.w.get(n))
        for n in writes:
            waits.append(self.w.get(n))
            waits.extend(self.r.get(n, ()))
        if dma:
            dsem = DLSEM[eng]
            self._op(eng, dsem, fn, waits, 16)
            if defer:
                pd = self.pending_dma.setdefault(eng, [])
                pd.extend(("w", n) for n in writes)
                pd.extend(("r", n) for n in reads)
                return None
            pend = self.pending_dma.pop(eng, [])
            tok = self._op(eng, eng, None, [(dsem, self.cnt[dsem])], 1)
            for kind, n in pend:
                if kind == "w":
                    self.w[n] = tok
                    self.r[n] = []
                else:
                    self.r.setdefault(n, []).append(tok)
        else:
            tok = self._op(eng, eng, fn, waits, 1)
        for n in reads:
            self.r.setdefault(n, []).append(tok)
        for n in writes:
            self.w[n] = tok
            self.r[n] = []
        return tok

    def dma_flush(self, eng):
        pend = self.pending_dma.pop(eng, None)
        if not pend:
            return None
        dsem = DLSEM[eng]
        tok = self._op(eng, eng, None, [(dsem, self.cnt[dsem])], 1)
        for kind, n in pend:
            if kind == "w":
                self.w[n] = tok
                self.r[n] = []
            else:
                self.r.setdefault(n, []).append(tok)
        return tok

    def token(self, sem):
        return (sem, self.cnt[sem])


def build_program(debug=False, stage=99):
    nc = bass.Bass("TRN2", target_bir_lowering=False, detect_race_conditions=False)

    dx = nc.dram_tensor("x", [T, D], F32, kind="ExternalInput").ap()
    d_win = nc.dram_tensor("w_in", [D, DIN], F32, kind="ExternalInput").ap()
    d_kx = nc.dram_tensor("kx", [128, 32], F32, kind="ExternalInput").ap()
    d_kz = nc.dram_tensor("kz", [128, 32], F32, kind="ExternalInput").ap()
    d_wxp = nc.dram_tensor("w_xproj", [D, 96], F32, kind="ExternalInput").ap()
    d_wdt = nc.dram_tensor("w_dt", [DTR, D], F32, kind="ExternalInput").ap()
    d_bdt = nc.dram_tensor("b_dt", [128, KC], F32, kind="ExternalInput").ap()
    d_alog = nc.dram_tensor("a_log", [128, 128], F32, kind="ExternalInput").ap()
    d_dd = nc.dram_tensor("d_d", [128, KC], F32, kind="ExternalInput").ap()
    d_wout = nc.dram_tensor("w_out", [DIN, D], F32, kind="ExternalInput").ap()
    d_bout = nc.dram_tensor("b_out", [1, D], F32, kind="ExternalInput").ap()
    d_rw = nc.dram_tensor("r_wT", [D, NE], F32, kind="ExternalInput").ap()
    d_we = nc.dram_tensor("w_e", [NE, D, D], F32, kind="ExternalInput").ap()
    d_c = nc.dram_tensor("c_row", [128, KC], F32, kind="ExternalInput").ap()
    d_wmod = nc.dram_tensor("wmod", [D, 6 * D], F32, kind="ExternalInput").ap()
    d_out = nc.dram_tensor("out", [T, D], F32, kind="ExternalOutput").ap()
    d_x2s = nc.dram_tensor("x2s", [T, D], F32, kind="Internal").ap()
    d_h2s = nc.dram_tensor("h2s", [T, D], F32R, kind="Internal").ap()
    d_tok = nc.dram_tensor("dtok", [NH * 6400, 1], I32, kind="Internal").ap()
    d_ctr = nc.dram_tensor("dctr", [NH * 6400, D], F32, kind="Internal").ap()
    d_werb = nc.dram_tensor("werb", [NE, D, D], BF16, kind="Internal").ap()
    d_winr = nc.dram_tensor("winr", [D, DIN], F32R, kind="Internal").ap()
    d_woutr = nc.dram_tensor("woutr", [DIN, D], F32R, kind="Internal").ap()
    eye = nc.inline_tensor(np.eye(128, dtype=np.float32), "eye").ap()
    toki_c = nc.inline_tensor(
        np.ascontiguousarray(np.arange(T, dtype=np.int32).reshape(NTT, 128).T
                             ).view(np.float32), "toki_c").ap()
    erow_c = nc.inline_tensor(
        (np.arange(NE, dtype=np.float32) * CAP
         + np.array([[0.0], [6400.0]], np.float32)), "erow_c").ap()

    dbg = {}
    if debug:
        for nm, shp in (("xh", [KC, 128, T]), ("z", [KC, 128, T]),
                        ("delta", [KC, 128, T]), ("y", [KC, 128, T]),
                        ("wm", [128, NTT, NE]), ("h2", [KC, 128, T]),
                        ("x2", [T, D]), ("hfm", [KC, 128, 768]),
                        ("mod", [128, 6 * D])):
            dbg[nm] = nc.dram_tensor("dbg_" + nm, shp, F32, kind="ExternalOutput").ap()

    p = Plan()

    def ga(name, shape, dt):
        return nc.alloc_sbuf_tensor(name, shape, dt)

    ones = ga("ones", [128, 128], F32R)
    identr = ga("identr", [128, 128], F32R)
    identf = ga("identf", [128, 128], F32)
    toki = ga("toki", [128, NTT], I32)
    erows = ga("erows", [128, 2, NE], F32)
    modB = ga("modB", [128, 3 * D], F32)
    rwt = ga("rwt", [128, KC, NE], F32R)
    cols = ga("cols", [128, 256], F32)
    stl = ga("stl", [128, 256], F32)

    def settle(eng, resources):
        if eng == "vector":
            p.do(eng, lambda e: e.tensor_copy(out=stl[:, 128:256], in_=stl[:, 0:128]),
                 writes=list(resources))
        else:
            p.do(eng, lambda e: e.copy(stl[:, 128:256], stl[:, 0:128]),
                 writes=list(resources))
    xtile = ga("xtile", [128, 2, D], F32)
    escr = ga("escr", [128, 2, D], F32)
    htm2 = ga("htm2", [128, 2, D], F32R)
    sqscr = ga("sqscr", [128, D], F32)
    recipc = ga("recipc", [128, NTT], F32)
    wmsk = ga("wmsk", [128, NTT, NE], F32)

    stE = contextlib.ExitStack()

    def ea(name, shape, dt):
        return stE.enter_context(nc.sbuf_tensor(name, shape, dt))

    modA = ea("modA", [128, 3 * D], F32)
    gab = ea("gab", [128, D], F32)
    kxc = ea("kxc", [128, 32], F32)
    kzc = ea("kzc", [128, 32], F32)
    aneg = ea("aneg", [128, 128], F32)
    bdt2 = ea("bdt2", [128, KC], F32)
    ddc = ea("ddc", [128, KC], F32)
    wxp = ea("wxp", [128, KC, 96], F32R)
    wdt = ea("wdt", [DTR, D], F32R)
    hstash = ea("hstash", [128, 512], F32)

    pAB = [nc.alloc_psum_tensor(nm, [128, 1024], F32) for nm in ("pA", "pB", "pC", "pD")]
    pA, pB, pC, pD = pAB
    tr0 = pD[:, 0:128].bitcast(F32R)
    tr1 = pD[:, 512:640].bitcast(F32R)

    def bankres(i):
        return ("ps%d_0" % i, "ps%d_1" % i)

    st0 = contextlib.ExitStack()
    silu_c = st0.enter_context(nc.sbuf_tensor("silu_c", [128, KC], F32))
    modw = st0.enter_context(nc.sbuf_tensor("modw", [128, KC, 128], F32R))
    wmbuf = st0.enter_context(nc.sbuf_tensor("wmbuf", [128, 4, 512], F32R))
    p.do("gpsimd", lambda e: e.dma_start(out=identr[:], in_=eye),
         writes=["identr"], dma=True)
    p.do("sync", lambda e: e.dma_start(out=identf[:], in_=eye),
         writes=["identf"], dma=True)
    p.do("sync", lambda e: e.dma_start(out=toki[:].bitcast(F32), in_=toki_c),
         writes=["toki"], dma=True)
    for hh in range(NH):
        p.do("sync", lambda e, hh=hh: e.dma_start(
            out=erows[:, hh, :],
            in_=erow_c[hh:hh + 1].unsqueeze(1).broadcast_to([1, 128, NE])),
            writes=["erows"], dma=True)
    p.do("gpsimd", lambda e: e.memset(ones[:].bitcast(F32), 1.0), writes=["ones"])
    p.do("sync", lambda e: e.dma_start(out=kxc[:], in_=d_kx), writes=["kxc"], dma=True)
    p.do("sync", lambda e: e.dma_start(out=kzc[:], in_=d_kz), writes=["kzc"], dma=True)
    p.do("sync", lambda e: e.dma_start(out=aneg[:], in_=d_alog), writes=["aneg"], dma=True)
    p.do("sync", lambda e: e.dma_start(out=bdt2[:], in_=d_bdt), writes=["bdt2"], dma=True)
    p.do("sync", lambda e: e.dma_start(out=ddc[:], in_=d_dd), writes=["ddc"], dma=True)
    p.do("sync", lambda e: e.dma_start(out=silu_c[:], in_=d_c),
        writes=["silu_c"], dma=True)
    p.do("sync", lambda e: e.dma_start(
        out=gab[:], in_=d_bout.unsqueeze(1).broadcast_to([1, 128, D])),
        writes=["gab"], dma=True)
    p.do("gpsimd", lambda e: e.dma_start(
        out=wxp[:], in_=d_wxp.rearrange("(c q) n -> q c n", q=128)),
        writes=["wxp"], dma=True)
    p.do("gpsimd", lambda e: e.dma_start(out=wdt[:], in_=d_wdt), writes=["wdt"], dma=True)
    p.do("gpsimd", lambda e: e.dma_start(
        out=rwt[:], in_=d_rw.rearrange("(c q) n -> q c n", q=128)),
        writes=["rwt"], dma=True)
    p.do("vector", lambda e: e.memset(stl[:], 0.0), writes=["stl"])
    p.do("vector", lambda e: e.memset(cols[:, 0:1], 1e-5), writes=["c_eps"])
    p.do("vector", lambda e: e.memset(cols[:, 1:2], 1.0), writes=["c_one"])

    p.do("scalar", lambda e: e.activation(out=aneg[:], in_=aneg[:], func=AF.Exp),
         reads=["aneg"], writes=["aneg"])
    p.do("vector", lambda e: e.tensor_scalar(out=aneg[:], in0=aneg[:], scalar1=-1.0,
                                             scalar2=None, op0=OP.mult),
         reads=["aneg"], writes=["aneg"])
    p.do("vector", lambda e: e.tensor_scalar(out=bdt2[:], in0=bdt2[:], scalar1=2.0,
                                             scalar2=None, op0=OP.mult),
         reads=["bdt2"], writes=["bdt2"])
    p.do("scalar", lambda e: e.activation(out=silu_c[:], in_=silu_c[:], func=AF.Silu),
         reads=["silu_c"], writes=["silu_c"])
    settle("scalar", ["silu_c"])
    for c in range(KC):
        p.do("vector", lambda e, c=c: e.tensor_scalar(
            out=modw[:, c, :], in0=ones[:].bitcast(F32), scalar1=silu_c[:, c:c + 1],
            scalar2=None, op0=OP.mult), reads=["ones", "silu_c"], writes=["modw"])

    for nb in range(12):
        pi = nb % 2
        pv = pAB[pi][:, 0:512]
        b0, b1 = bankres(pi)
        for c in range(0, KC, 2):
            for cc in (c, c + 1):
                par = (nb * KC + cc) % 4
                p.do("gpsimd", lambda e, cc=cc, nb=nb, par=par: e.dma_start(
                    out=wmbuf[:, par, :],
                    in_=d_wmod[cc * 128:(cc + 1) * 128, nb * 512:(nb + 1) * 512]),
                    writes=["wmbuf%d" % par], dma=True, defer=True)
            p.dma_flush("gpsimd")
            for cc in (c, c + 1):
                par = (nb * KC + cc) % 4
                pre = (list(p.r.get(b0, ())) + [p.w.get(b0)]) if cc == 0 else []
                p.do("tensor", lambda e, cc=cc, par=par, pv=pv: e.matmul(
                    pv, modw[:, cc, :], wmbuf[:, par, :],
                    start=(cc == 0), stop=(cc == KC - 1)),
                    reads=["modw", "wmbuf%d" % par], extra=pre,
                    writes=([b0] if cc == KC - 1 else []))
        mt = modA if nb < 6 else modB
        mr = "modA" if nb < 6 else "modB"
        mo = (nb % 6) * 512
        p.do("vector", lambda e, mt=mt, mo=mo, pv=pv: e.tensor_copy(
            out=mt[:, mo:mo + 512], in_=pv),
            reads=[b0], writes=[mr])

    def modsl(i):
        t = modA if i < 3 else modB
        return t[:, (i % 3) * D:(i % 3 + 1) * D]

    def modres(i):
        return "modA" if i < 3 else "modB"

    for i in (1, 4):
        p.do("vector", lambda e, i=i: e.tensor_scalar(
            out=modsl(i), in0=modsl(i), scalar1=1.0, scalar2=None, op0=OP.add),
            reads=[modres(i)], writes=[modres(i)])
    for gi in (2, 5):
        p.do("vector", lambda e, gi=gi: e.tensor_scalar(
            out=sqscr[:], in0=modsl(gi), scalar1=-1.0, scalar2=None, op0=OP.mult),
            reads=[modres(gi)], writes=["sqscr"])
        p.do("vector", lambda e, gi=gi: e.tensor_tensor(
            out=sqscr[:], in0=sqscr[:], in1=modsl(gi), op=OP.max),
            reads=["sqscr", modres(gi)], writes=["sqscr"])
        p.do("vector", lambda e, gi=gi: e.tensor_scalar(
            out=sqscr[:], in0=sqscr[:], scalar1=0.25, scalar2=None, op0=OP.max),
            reads=["sqscr"], writes=["sqscr"])
        p.do("vector", lambda e, gi=gi: e.tensor_scalar(
            out=escr[:, 0, :], in0=modsl(gi), scalar1=0.0, scalar2=None, op0=OP.is_ge),
            reads=[modres(gi)], writes=["escr0"])
        p.do("vector", lambda e, gi=gi: e.tensor_scalar(
            out=escr[:, 0, :], in0=escr[:, 0, :], scalar1=2.0, scalar2=-1.0,
            op0=OP.mult, op1=OP.add), reads=["escr0"], writes=["escr0"])
        p.do("vector", lambda e, gi=gi: e.tensor_tensor(
            out=modsl(gi), in0=escr[:, 0, :], in1=sqscr[:], op=OP.mult),
            reads=["escr0", "sqscr"], writes=[modres(gi)])
    p.do("vector", lambda e: e.tensor_tensor(out=gab[:], in0=gab[:], in1=modsl(2),
                                             op=OP.mult),
         reads=["gab", "modA"], writes=["gab"])
    if debug:
        p.do("sync", lambda e: e.dma_start(out=dbg["mod"][:, 0:3 * D], in_=modA[:]),
             reads=["modA"], dma=True)
        p.do("sync", lambda e: e.dma_start(out=dbg["mod"][:, 3 * D:6 * D], in_=modB[:]),
             reads=["modB"], dma=True)

    p.do("gpsimd", lambda e: e.dma_start(out=d_winr, in_=d_win), writes=["winr"],
         dma=True, defer=True)
    p.do("gpsimd", lambda e: e.dma_start(out=d_woutr, in_=d_wout), writes=["woutr"],
         dma=True, defer=True)
    p.dma_flush("gpsimd")
    p.do("scalar", lambda e: e.dma_start(
        out=d_tok[:, 0].rearrange("(b a) -> b a", b=128),
        in_=stl[:, 0:100].bitcast(I32)),
        reads=["stl"], writes=["dtok0", "dtok1"], dma=True)
    st0.close()
    p.fence()
    for ei in range(NE):
        p.do("gpsimd", lambda e, ei=ei: e.dma_start(
            out=d_werb[ei], in_=d_we[ei]),
            writes=["werb%d" % ei], dma=True, defer=True)
    if stage < 1:
        return _emit(nc, p, None)

    def ln_batch(tiles, src_dram, sh_sl, sc_sl, hdst_fn_of, hdst_res, mod_res,
                 interleave=None, h2s=False):
        def cb(j):
            return 16 + 8 * j

        for j0 in range(0, len(tiles), 2):
            pair = [(j, tiles[j]) for j in range(j0, min(j0 + 2, len(tiles)))]
            for j, tt in pair:
                par = tt % 2
                p.do("sync", lambda e, tt=tt, par=par: e.dma_start(
                    out=xtile[:, par, :], in_=src_dram[tt * 128:(tt + 1) * 128, :]),
                    writes=["xtile%d" % par], dma=True, defer=True)
            p.dma_flush("sync")
            for j, tt in pair:
                par = tt % 2
                p.do("vector", lambda e, j=j, par=par: e.tensor_reduce(
                    out=cols[:, cb(j):cb(j) + 1], in_=xtile[:, par, :], axis=AX.X,
                    op=OP.add, negate=True), reads=["xtile%d" % par],
                    writes=["cL%d" % j])
                p.do("scalar", lambda e, j=j, par=par: e.activation(
                    out=sqscr[:], in_=xtile[:, par, :], func=AF.Square,
                    accum_out=cols[:, cb(j) + 2:cb(j) + 3]),
                    reads=["xtile%d" % par], writes=["sqscr", "cQ%d" % j])
        settle("vector", ["cL%d" % j for j in range(len(tiles))])
        for j in range(len(tiles)):
            p.do("scalar", lambda e, j=j: e.mul(
                cols[:, cb(j) + 1:cb(j) + 2], cols[:, cb(j):cb(j) + 1], 1.0 / D),
                reads=["cL%d" % j], writes=["cM%d" % j])
        for j in range(len(tiles)):
            p.do("scalar", lambda e, j=j: e.activation(
                out=cols[:, cb(j) + 3:cb(j) + 4], in_=cols[:, cb(j) + 1:cb(j) + 2],
                func=AF.Square), reads=["cM%d" % j], writes=["c2%d" % j])
        for j in range(len(tiles)):
            p.do("scalar", lambda e, j=j: e.mul(
                cols[:, cb(j) + 4:cb(j) + 5], cols[:, cb(j) + 3:cb(j) + 4], -1.0),
                reads=["c2%d" % j], writes=["c3%d" % j])
        for j in range(len(tiles)):
            p.do("scalar", lambda e, j=j: e.add(
                cols[:, cb(j) + 4:cb(j) + 5], cols[:, cb(j) + 4:cb(j) + 5],
                cols[:, 0:1]), reads=["c3%d" % j, "c_eps"], writes=["c3%d" % j])
        for j in range(len(tiles)):
            p.do("scalar", lambda e, j=j: e.activation(
                out=cols[:, cb(j) + 5:cb(j) + 6], in_=cols[:, cb(j) + 2:cb(j) + 3],
                func=AF.Ln, scale=1.0 / D, bias=cols[:, cb(j) + 4:cb(j) + 5]),
                reads=["cQ%d" % j, "c3%d" % j], writes=["cS%d" % j])
        for j in range(len(tiles)):
            p.do("scalar", lambda e, j=j: e.activation(
                out=cols[:, cb(j) + 6:cb(j) + 7], in_=cols[:, cb(j) + 5:cb(j) + 6],
                func=AF.Exp, scale=-0.5), reads=["cS%d" % j], writes=["cR%d" % j])
        for j, tt in enumerate(tiles):
            par = tt % 2
            hsl = tt % 2 if h2s else 0
            htm = htm2[:, hsl, :]
            hres = "htm%d" % hsl
            eres = "escr%d" % par
            xres = "xtile%d" % par
            p.do("sync", lambda e, tt=tt, par=par: e.dma_start(
                out=xtile[:, par, :], in_=src_dram[tt * 128:(tt + 1) * 128, :]),
                writes=[xres], dma=True)
            p.do("scalar", lambda e, j=j, par=par: e.add(
                escr[:, par, :], xtile[:, par, :], cols[:, cb(j) + 1:cb(j) + 2]),
                reads=[xres, "cM%d" % j], writes=[eres])
            p.do("vector", lambda e, j=j, par=par: e.scalar_tensor_tensor(
                out=escr[:, par, :], in0=escr[:, par, :],
                scalar=cols[:, cb(j) + 6:cb(j) + 7], in1=sc_sl,
                op0=OP.mult, op1=OP.mult),
                reads=[eres, "cR%d" % j, mod_res], writes=[eres])
            p.do("vector", lambda e, par=par, htm=htm: e.tensor_tensor(
                out=htm[:], in0=escr[:, par, :], in1=sh_sl, op=OP.add),
                reads=[eres, mod_res], writes=[hres])
            if h2s:
                p.do("scalar", lambda e, tt=tt, htm=htm: e.dma_start(
                    out=d_h2s[tt * 128:(tt + 1) * 128, :], in_=htm[:]),
                    reads=[hres], writes=["h2s"], dma=True)
            hdst_fn = hdst_fn_of(tt)
            for c in range(KC):
                trb = "ps3_0" if c % 2 == 0 else "ps3_1"
                trv = tr0 if c % 2 == 0 else tr1
                p.do("tensor", lambda e, c=c, trv=trv, htm=htm: e.transpose(
                    trv, htm[:, c * 128:(c + 1) * 128], identr[:]),
                    reads=[hres, "identr"], writes=[trb])
                dst, s0, w = hdst_fn(c)
                if c % 2 == 0:
                    p.do("vector", lambda e, dst=dst, trv=trv, s0=s0, w=w:
                         e.tensor_copy(out=dst, in_=trv.bitcast(F32)[:, s0:s0 + w]),
                         reads=[trb], writes=[hdst_res])
                else:
                    p.do("scalar", lambda e, dst=dst, trv=trv, s0=s0, w=w: e.copy(
                        dst, trv.bitcast(F32)[:, s0:s0 + w]),
                        reads=[trb], writes=[hdst_res])
            if interleave is not None:
                interleave(j)

    stA = contextlib.ExitStack()

    def qa(name, shape, dt):
        return stA.enter_context(nc.sbuf_tensor(name, shape, dt))

    hfm = qa("hfm", [128, KC, 528], F32R)
    rawb = qa("rawb", [128, 2, 520], F32)
    convs = qa("convs", [128, 2, TQ], F32)
    xh = qa("xh", [128, KC, TQ], F32R)
    zs = qa("zs", [128, KC, TQ], F32R)
    delta = qa("delta", [128, KC, TQ], F32)
    yq = qa("yq", [128, KC, TQ], F32R)
    dAs = qa("dAs", [128, 2, TQ], F32)
    bus = qa("bus", [128, 4, TQ], F32)
    hsc = qa("hsc", [128, 2, TQ], F32)
    bbc = qa("bbc", [128, 2, TQ], F32)
    cbc = qa("cbc", [128, 2, TQ], F32)
    xdbl = qa("xdbl", [128, 4, 96], F32R)
    dtT = qa("dtT", [DTR, TQ], F32R)
    bcT = qa("bcT", [32, TQ], F32R)
    winb = qa("winb", [128, 2, KC, 128], F32R)
    woutb = qa("woutb", [128, 2, D], F32R)

    NQ_RUN = NQ if stage >= 6 else 1
    wevac = rawb.ap().rearrange("p a b -> p (a b)")[:, 0:D]

    def emit_ln_q(q):
        qs = q * TQ
        qe = qs + TQ
        lo_tile = max(qs // 128 - 1, 0)
        hi_tile = min(qe // 128 + 1, NTT)
        win0 = qs - 8
        win1 = qe + 8

        def hdst_of(tt, win0=win0, win1=win1):
            a = max(tt * 128, win0)
            b = min(tt * 128 + 128, win1)
            off = a - win0
            s0 = a - tt * 128
            w = b - a

            def hdst(c, off=off, s0=s0, w=w):
                return hfm[:, c, off:off + w], 0 + s0, w
            return hdst
        ln_batch(list(range(lo_tile, hi_tile)), dx, modsl(0), modsl(1),
                 hdst_of, "hfm", "modA")

    emit_ln_q(0)
    for q in range(NQ_RUN):
        qs = q * TQ
        qe = qs + TQ
        win0 = qs - 8

        if debug:
            for c in range(KC):
                p.do("sync", lambda e, c=c: e.dma_start(
                    out=dbg["hfm"][c, :, 8:520],
                    in_=hfm[:, c, 8:520].bitcast(F32)),
                    reads=["hfm"], dma=True)
        if stage < 2:
            stA.close()
            return _emit(nc, p, None)

        r0 = max(qs - 1, 0)
        r1 = min(qe + 2, T)
        L = r1 - r0
        c0 = r0 - win0
        LM = min(L, 512)
        LH = (L - LM + 1) // 2 * 2
        i0 = r0 - qs + 2
        if q == NQ - 1:
            for c in range(KC):
                p.do("vector", lambda e, c=c: e.memset(
                    hfm[:, c, 520:524].bitcast(F32), 0.0), writes=["hfm"])
        for m in range(16):
            par = m % 2
            pv = pAB[par]
            b0, b1 = bankres(par)
            p.do("sync", lambda e, m=m, par=par: e.dma_start(
                out=winb[:, par], in_=d_winr[:, m * 128:(m + 1) * 128].rearrange(
                    "(k q) mm -> q k mm", q=128)),
                reads=["winr"], writes=["winb%d" % par], dma=True)
            def win_mms(e, par=par, pv=pv, c0=c0, LM=LM, LH=LH):
                for k in range(KC):
                    mm = e.matmul(pv[:, 0:LM], winb[:, par, k, :],
                                  hfm[:, k, c0:c0 + LM],
                                  start=(k == 0), stop=(k == KC - 1))
                for k in range(KC):
                    mm = e.matmul(pv[:, 512:512 + LH], winb[:, par, k, :],
                                  hfm[:, k, c0 + LM:c0 + LM + LH],
                                  start=(k == 0), stop=(k == KC - 1))
                return mm
            p.do("tensor", win_mms, reads=["winb%d" % par, "hfm"], writes=[b0, b1])
            rres = "rawb%d" % par
            p.do("vector", lambda e, par=par, pv=pv, i0=i0, LM=LM: e.tensor_copy(
                out=rawb[:, par, i0:i0 + LM], in_=pv[:, 0:LM]),
                reads=[b0], writes=[rres])
            p.do("vector", lambda e, par=par, pv=pv, i0=i0, LM=LM, LH=LH: e.tensor_copy(
                out=rawb[:, par, i0 + LM:i0 + LM + LH], in_=pv[:, 512:512 + LH]),
                reads=[b1], writes=[rres])
            if q == 0:
                p.do("vector", lambda e, par=par: e.memset(rawb[:, par, 1:2], 0.0),
                     writes=[rres])
            if q == NQ - 1:
                p.do("vector", lambda e, par=par: e.memset(rawb[:, par, 514:517], 0.0),
                     writes=[rres])
            kcol = kxc if m < KC else kzc
            kres = "kxc" if m < KC else "kzc"
            ccol = (m % KC) * 4
            cres = "convs%d" % par
            p.do("vector", lambda e, par=par, kcol=kcol, ccol=ccol: e.tensor_scalar(
                out=convs[:, par, :], in0=rawb[:, par, 1:513],
                scalar1=kcol[:, ccol:ccol + 1], scalar2=None, op0=OP.mult),
                reads=[rres, kres], writes=[cres])
            for j in (1, 2, 3):
                p.do("vector", lambda e, par=par, kcol=kcol, ccol=ccol, j=j:
                     e.scalar_tensor_tensor(
                         out=convs[:, par, :], in0=rawb[:, par, 1 + j:513 + j],
                         scalar=kcol[:, ccol + j:ccol + j + 1], in1=convs[:, par, :],
                         op0=OP.mult, op1=OP.add),
                     reads=[rres, kres, cres], writes=[cres])
            dstt = xh[:, m % KC] if m < KC else zs[:, m % KC]
            dres = ("xh%d" % (m % KC)) if m < KC else ("zs%d" % (m % KC))
            p.do("scalar", lambda e, dstt=dstt, par=par: e.activation(
                out=dstt, in_=convs[:, par, :], func=AF.Silu),
                reads=[cres], writes=[dres])

        if debug:
            for c in range(KC):
                p.do("sync", lambda e, c=c, qs=qs, qe=qe: e.dma_start(
                    out=dbg["xh"][c, :, qs:qe], in_=xh[:, c].bitcast(F32)),
                    reads=["xh%d" % c], dma=True)
                p.do("sync", lambda e, c=c, qs=qs, qe=qe: e.dma_start(
                    out=dbg["z"][c, :, qs:qe], in_=zs[:, c].bitcast(F32)),
                    reads=["zs%d" % c], dma=True)
        if stage < 3:
            stA.close()
            return _emit(nc, p, None)
        for i in range(4):
            b0, b1 = bankres(2)

            def xp_mms(e, i=i):
                for c in range(KC):
                    mm = e.matmul(pC[:, 0:96], xh[:, c, i * 128:(i + 1) * 128],
                                  wxp[:, c, :], start=(c == 0), stop=(c == KC - 1))
                return mm
            p.do("tensor", xp_mms,
                 reads=["wxp"] + ["xh%d" % c for c in range(KC)], writes=[b0])
            p.do("vector", lambda e, i=i: e.tensor_copy(
                out=xdbl[:, i, :], in_=pC[:, 0:96]),
                reads=[b0], writes=["xdbl%d" % i])
        for i in range(4):
            p.do("tensor", lambda e, i=i: e.transpose(tr0[0:64, :], xdbl[:, i, 0:64], identr[:]),
                 reads=["xdbl%d" % i, "identr"], writes=["ps3_0"])
            p.do("vector", lambda e, i=i: e.tensor_copy(
                out=dtT[:, i * 128:(i + 1) * 128],
                in_=tr0.bitcast(F32)[0:64, :]), reads=["ps3_0"], writes=["dtT"])
            p.do("tensor", lambda e, i=i: e.transpose(tr1[0:32, :], xdbl[:, i, 64:96], identr[:]),
                 reads=["xdbl%d" % i, "identr"], writes=["ps3_1"])
            p.do("vector", lambda e, i=i: e.tensor_copy(
                out=bcT[:, i * 128:(i + 1) * 128],
                in_=tr1.bitcast(F32)[0:32, :]), reads=["ps3_1"], writes=["bcT"])

        for c in range(KC):
            half = c % 2
            pv = pC[:, 0:512] if half == 0 else pC[:, 512:1024]
            bn = bankres(2)[half]
            p.do("tensor", lambda e, c=c, pv=pv: e.matmul(
                pv, wdt[:, c * 128:(c + 1) * 128], dtT[:], start=True, stop=True),
                reads=["dtT", "wdt"], writes=[bn])
            p.do("scalar", lambda e, c=c, pv=pv: e.activation(
                out=convs[:, 0, :], in_=pv, func=AF.Exp, bias=bdt2[:, c:c + 1]),
                reads=[bn, "bdt2", "convs0"], writes=["convs0"])
            p.do("scalar", lambda e, c=c: e.activation(
                out=delta[:, c], in_=convs[:, 0, :], func=AF.Ln, bias=cols[:, 1:2]),
                reads=["convs0", "c_one"], writes=["delta%d" % c])
            p.do("vector", lambda e, c=c: e.tensor_scalar(
                out=yq[:, c], in0=xh[:, c].bitcast(F32),
                scalar1=ddc[:, c:c + 1], scalar2=None, op0=OP.mult),
                reads=["xh%d" % c, "ddc"], writes=["yq%d" % c])
            p.do("vector", lambda e, c=c: e.tensor_tensor(
                out=xh[:, c], in0=delta[:, c], in1=xh[:, c].bitcast(F32),
                op=OP.mult), reads=["delta%d" % c, "xh%d" % c], writes=["xh%d" % c])

        if debug:
            for c in range(KC):
                p.do("sync", lambda e, c=c, qs=qs, qe=qe: e.dma_start(
                    out=dbg["delta"][c, :, qs:qe], in_=delta[:, c]),
                    reads=["delta%d" % c], dma=True)
        if stage < 4:
            stA.close()
            return _emit(nc, p, None)
        def bc_dma(n):
            bp = n % 2
            p.do("sync", lambda e, n=n, bp=bp: e.dma_start(
                out=bbc[:, bp, :],
                in_=bcT.ap().bitcast(F32)[n:n + 1, :].unsqueeze(1)
                    .broadcast_to([1, 128, TQ])),
                reads=["bcT"], writes=["bbc%d" % bp], dma=True, defer=True)
            p.do("sync", lambda e, n=n, bp=bp: e.dma_start(
                out=cbc[:, bp, :],
                in_=bcT.ap().bitcast(F32)[16 + n:17 + n, :].unsqueeze(1)
                    .broadcast_to([1, 128, TQ])),
                reads=["bcT"], writes=["cbc%d" % bp], dma=True, defer=True)
            p.dma_flush("sync")
        bc_dma(0)
        for n in range(NST):
            if n + 1 < NST:
                bc_dma(n + 1)
            for c in range(0, KC, 2):
                i = n * KC + c
                pk = (i // 2) % 3
                if pk == 2:
                    buap = convs[:, 0:2, :]
                    bur = ["convs0", "convs1"]
                else:
                    buap = bus[:, 2 * pk:2 * pk + 2, :]
                    bur = ["bus%d" % (2 * pk), "bus%d" % (2 * pk + 1)]
                for dc in (0, 1):
                    cn = (c + dc) * NST + n
                    sp = (i + dc) % 2
                    p.do("scalar", lambda e, c=c, dc=dc, cn=cn, sp=sp: e.activation(
                        out=dAs[:, sp, :], in_=delta[:, c + dc], func=AF.Exp,
                        scale=aneg[:, cn:cn + 1]),
                        reads=["delta%d" % (c + dc), "aneg"], writes=["dAs%d" % sp])
                p.do("gpsimd", lambda e, c=c, buap=buap, bp=n % 2: e.tensor_tensor(
                    out=buap, in0=xh[:, c:c + 2].bitcast(F32),
                    in1=bbc[:, bp, :].unsqueeze(1).broadcast_to([128, 2, TQ]),
                    op=OP.mult),
                    reads=["xh%d" % c, "xh%d" % (c + 1), "bbc%d" % (n % 2)],
                    writes=bur)
                for dc in (0, 1):
                    cn = (c + dc) * NST + n
                    sp = (i + dc) % 2
                    init = 0.0 if q == 0 else hstash[:, cn * 4 + 3:cn * 4 + 4]
                    rds = ["dAs%d" % sp, bur[dc]]
                    if q > 0:
                        rds.append("hst%d" % cn)
                    p.do("vector", lambda e, sp=sp, buap=buap, dc=dc, init=init:
                         e.tensor_tensor_scan(
                             out=hsc[:, sp, :], data0=dAs[:, sp, :],
                             data1=buap[:, dc, :],
                             initial=init, op0=OP.mult, op1=OP.add),
                         reads=rds, writes=["hsc%d" % sp])
                    p.do("scalar", lambda e, sp=sp, cn=cn: e.copy(
                        hstash[:, cn * 4:cn * 4 + 4], hsc[:, sp, TQ - 4:TQ]),
                        reads=["hsc%d" % sp], writes=["hst%d" % cn])
                p.do("vector", lambda e, buap=buap, bp=n % 2: e.tensor_tensor(
                    out=buap, in0=hsc[:, :, :],
                    in1=cbc[:, bp, :].unsqueeze(1).broadcast_to([128, 2, TQ]),
                    op=OP.mult),
                    reads=["hsc0", "hsc1", "cbc%d" % (n % 2)],
                    writes=bur)
                p.do("vector", lambda e, c=c, buap=buap: e.tensor_tensor(
                    out=yq[:, c:c + 2], in0=yq[:, c:c + 2].bitcast(F32),
                    in1=buap, op=OP.add),
                    reads=["yq%d" % c, "yq%d" % (c + 1)] + bur,
                    writes=["yq%d" % c, "yq%d" % (c + 1)])

        if debug:
            for c in range(KC):
                p.do("sync", lambda e, c=c, qs=qs, qe=qe: e.dma_start(
                    out=dbg["y"][c, :, qs:qe], in_=yq[:, c].bitcast(F32)),
                    reads=["yq%d" % c], dma=True)

        if stage < 5:
            stA.close()
            return _emit(nc, p, None)
        if q + 1 < NQ_RUN:
            emit_ln_q(q + 1)
        for k in range(16):
            par = k % 2
            src = yq if k < KC else zs
            sres = ("yq%d" % (k % KC)) if k < KC else ("zs%d" % (k % KC))
            cc = k % KC
            p.do("scalar", lambda e, k=k, par=par: e.dma_start(
                out=woutb[:, par, :], in_=d_woutr[k * 128:(k + 1) * 128, :]),
                reads=["woutr"], writes=["woutb%d" % par], dma=True)
            def wo_mms(e, k=k, par=par, src=src, cc=cc):
                for i in range(4):
                    for nn in range(2):
                        mm = e.matmul(pAB[i][:, nn * 512:(nn + 1) * 512],
                                      src[:, cc, i * 128:(i + 1) * 128],
                                      woutb[:, par, nn * 512:(nn + 1) * 512],
                                      start=(k == 0), stop=(k == 15))
                return mm
            allb = [bn for i in range(4) for bn in bankres(i)]
            p.do("tensor", wo_mms, reads=["woutb%d" % par, sres],
                 writes=(allb if k == 0 else allb))
        for i in range(4):
            tt = qs // 128 + i
            par = i % 2
            b0, b1 = bankres(i)
            xres = "xtile%d" % par
            p.do("sync", lambda e, tt=tt, par=par: e.dma_start(
                out=xtile[:, par, :], in_=dx[tt * 128:(tt + 1) * 128, :]),
                writes=[xres], dma=True)
            p.do("vector", lambda e, i=i: e.tensor_tensor(
                out=wevac, in0=pAB[i][:], in1=modsl(2), op=OP.mult),
                reads=[b0, b1, "modA", "rawb0", "rawb1"],
                writes=["rawb0", "rawb1"])
            p.do("vector", lambda e: e.tensor_tensor(
                out=wevac, in0=wevac, in1=gab[:], op=OP.add),
                reads=["rawb0", "rawb1", "gab"], writes=["rawb0", "rawb1"])
            p.do("vector", lambda e, par=par: e.tensor_tensor(
                out=xtile[:, par, :], in0=xtile[:, par, :], in1=wevac,
                op=OP.add), reads=[xres, "rawb0", "rawb1"], writes=[xres])
            p.do("sync", lambda e, tt=tt, par=par: e.dma_start(
                out=d_x2s[tt * 128:(tt + 1) * 128, :], in_=xtile[:, par, :]),
                reads=[xres], writes=["x2s"], dma=True)
            if debug:
                p.do("sync", lambda e, tt=tt, par=par: e.dma_start(
                    out=dbg["x2"][tt * 128:(tt + 1) * 128, :], in_=xtile[:, par, :]),
                    reads=[xres], dma=True)

    stA.close()
    stE.close()
    p.fence()
    if stage < 7:
        return _emit(nc, p, None)

    stB = contextlib.ExitStack()

    def ba(name, shape, dt):
        return stB.enter_context(nc.sbuf_tensor(name, shape, dt))

    h2fm = ba("h2fm", [128, KC, TH], F32R)
    web2 = ba("web2", [128, 2, KC, D], BF16)
    lhsT = ba("lhsT", [128, 2, KC, CAP], BF16)
    gbA = ba("gbA", [128, 2, D], F32R)
    gbB = ba("gbB", [64, 2, D], F32R)
    gct = ba("gct", [128, 2, 2, D], F32)
    wmT = ba("wmT", [NE, TH], F32)
    ztE = ba("ztE", [NE, TH], F32)
    ppf = ba("ppf", [NE, TH], F32)
    ptokb = ba("ptokb", [128, KC, NE], F32)
    growb = ba("growb", [128, KC, NE], F32)
    selb = ba("selb", [128, KC, NE], F32)
    scrb = ba("scrb", [128, KC, NE], F32)
    colsb = ba("colsb", [128, KC, 8], F32)
    cidx = ba("cidx", [128, NTT, 2], I32)
    wab = ba("wab", [128, NTT, 2], F32)
    gi32 = ba("gi32", [128, 2, 2], I32)
    ctrb = ba("ctrb", [128, 2, D], F32)
    ctrb1 = ba("ctrb1", [64, 2, D], F32)
    p.dma_flush("gpsimd")
    p.do("vector", lambda e: e.memset(ztE[:], 0.0), writes=["ztE"])

    for pe_ in (0, 1):
        p.do("sync" if pe_ == 0 else "scalar", lambda e, ei=pe_, par=pe_:
             e.dma_start(out=web2[:, par], in_=d_werb[ei:ei + 1].rearrange(
                 "o (c q) n -> q (o c) n", q=128)),
             reads=["werb%d" % pe_], writes=["web%d" % pe_], dma=True)

    def emit_ln_router(hh):
        hs0 = hh * TH

        def h2dst_of(tt, hs0=hs0):
            off = (tt - hs0 // 128) * 128

            def hdst(c, off=off):
                return h2fm[:, c, off:off + 128], 0, 128
            return hdst
        tiles = [hs0 // 128 + i for i in range(8)]
        ln_batch(tiles, d_x2s, modsl(3), modsl(4), h2dst_of, "h2fm", "modB",
                 h2s=True)

        RB = [("ps2_0", pC[:, 0:NE]), ("ps2_1", pC[:, 512:512 + NE]),
              ("ps3_0", pD[:, 0:NE]), ("ps3_1", pD[:, 512:512 + NE])]
        for g in range(2):
            idxs = list(range(g * 4, (g + 1) * 4))
            for j, i in enumerate(idxs):
                off = i * 128
                bn, pv = RB[j]

                def rt_mms(e, off=off, pv=pv):
                    for c in range(KC):
                        mm = e.matmul(pv, h2fm[:, c, off:off + 128],
                                      rwt[:, c, :], start=(c == 0),
                                      stop=(c == KC - 1))
                    return mm
                p.do("tensor", rt_mms, reads=["h2fm", "rwt"], writes=[bn])
            for j, i in enumerate(idxs):
                bn, pv = RB[j]
                p.do("vector", lambda e, j=j, pv=pv: e.tensor_reduce(
                    out=cols[:, 80 + j:81 + j], in_=pv, axis=AX.X, op=OP.max,
                    negate=True), reads=[bn], writes=["cX%d" % j])
            settle("vector", ["cX%d" % j for j in range(4)])
            for j, i in enumerate(idxs):
                tt = hs0 // 128 + i
                bn, pv = RB[j]
                p.do("scalar", lambda e, j=j, tt=tt, pv=pv: e.activation(
                    out=wmsk[:, tt, :], in_=pv, func=AF.Exp,
                    bias=cols[:, 80 + j:81 + j], accum_out=cols[:, 84 + j:85 + j]),
                    reads=[bn, "cX%d" % j], writes=["wm%d" % tt, "cY%d" % j])
            settle("scalar", ["cY%d" % j for j in range(4)] +
                   ["wm%d" % (hs0 // 128 + i) for i in idxs])
            for j, i in enumerate(idxs):
                p.do("scalar", lambda e, j=j: e.activation(
                    out=cols[:, 88 + j:89 + j], in_=cols[:, 84 + j:85 + j],
                    func=AF.Ln), reads=["cY%d" % j], writes=["cZ%d" % j])
            for j, i in enumerate(idxs):
                tt = hs0 // 128 + i
                p.do("scalar", lambda e, j=j, tt=tt: e.activation(
                    out=recipc[:, tt:tt + 1], in_=cols[:, 88 + j:89 + j],
                    func=AF.Exp, scale=-1.0), reads=["cZ%d" % j], writes=["recip"])
            for j, i in enumerate(idxs):
                tt = hs0 // 128 + i
                p.do("vector", lambda e, j=j, tt=tt: e.max(
                    out=cols[:, 92 + 8 * j:100 + 8 * j], in_=wmsk[:, tt, :]),
                    reads=["wm%d" % tt], writes=["c8%d" % j])
            settle("vector", ["c8%d" % j for j in range(4)])
            for j, i in enumerate(idxs):
                tt = hs0 // 128 + i
                p.do("vector", lambda e, j=j, tt=tt: e.tensor_scalar(
                    out=cols[:, 128 + 32 * j:160 + 32 * j], in0=wmsk[:, tt, :],
                    scalar1=cols[:, 93 + 8 * j:94 + 8 * j],
                    scalar2=None, op0=OP.is_ge),
                    reads=["wm%d" % tt, "c8%d" % j], writes=["cK%d" % j])
            settle("vector", ["cK%d" % j for j in range(4)])
            for j, i in enumerate(idxs):
                tt = hs0 // 128 + i
                p.do("vector", lambda e, j=j, tt=tt: e.tensor_tensor(
                    out=wmsk[:, tt, :], in0=wmsk[:, tt, :],
                    in1=cols[:, 128 + 32 * j:160 + 32 * j],
                    op=OP.mult), reads=["wm%d" % tt, "cK%d" % j], writes=["wm%d" % tt])
        if debug:
            for i in range(8):
                tt = hs0 // 128 + i
                off = i * 128
                p.do("sync", lambda e, tt=tt: e.dma_start(
                    out=dbg["wm"][:, tt, :], in_=wmsk[:, tt, :]),
                    reads=["wm%d" % tt], dma=True)
                for c in range(KC):
                    p.do("sync", lambda e, c=c, off=off, hs0=hs0: e.dma_start(
                        out=dbg["h2"][c, :, hs0 + off:hs0 + off + 128],
                        in_=h2fm[:, c, off:off + 128].bitcast(F32)),
                        reads=["h2fm"], dma=True)

    def emit_index(hh):
        hb = hh * KC
        for j in range(KC):
            tt = hb + j
            bn = "ps2_0" if j < 4 else "ps2_1"
            pvw = pC[0:NE, j * 128:(j + 1) * 128]
            p.do("tensor", lambda e, tt=tt, pvw=pvw: e.transpose(
                pvw, wmsk[:, tt, :], identf[:]),
                reads=["wm%d" % tt, "identf"], writes=[bn])
        for j in range(KC):
            bn = "ps2_0" if j < 4 else "ps2_1"
            pvw = pC[0:NE, j * 128:(j + 1) * 128]
            eng2 = "vector" if j % 2 == 0 else "scalar"
            if eng2 == "vector":
                p.do(eng2, lambda e, j=j, pvw=pvw: e.tensor_copy(
                    out=wmT[:, j * 128:(j + 1) * 128], in_=pvw),
                    reads=[bn], writes=["wmT"])
            else:
                p.do(eng2, lambda e, j=j, pvw=pvw: e.copy(
                    wmT[:, j * 128:(j + 1) * 128], pvw),
                    reads=[bn], writes=["wmT"])
        p.do("vector", lambda e: e.tensor_scalar(
            out=wmT[:], in0=wmT[:], scalar1=0.0, scalar2=None, op0=OP.is_gt),
            reads=["wmT"], writes=["wmT"])
        p.do("vector", lambda e: e.tensor_tensor_scan(
            out=ppf[:], data0=ztE[:], data1=wmT[:], initial=0.0,
            op0=OP.add, op1=OP.add), reads=["ztE", "wmT"], writes=["ppf"])
        for j in range(KC):
            p.do("tensor", lambda e, j=j: e.transpose(
                pD[0:128, 256 + j * NE:256 + (j + 1) * NE],
                ppf[:, j * 128:(j + 1) * 128], identf[0:NE, 0:NE]),
                reads=["ppf", "identf"], writes=["ps3_0"])
        for j in range(KC):
            eng2 = "vector" if j % 2 == 0 else "scalar"
            pvw = pD[0:128, 256 + j * NE:256 + (j + 1) * NE]
            if eng2 == "vector":
                p.do(eng2, lambda e, j=j, pvw=pvw: e.tensor_copy(
                    out=ptokb[:, j, :], in_=pvw), reads=["ps3_0"], writes=["ixs"])
            else:
                p.do(eng2, lambda e, j=j, pvw=pvw: e.copy(
                    ptokb[:, j, :], pvw), reads=["ps3_0"], writes=["ixs"])
        for j in range(KC):
            p.do("vector", lambda e, j=j: e.tensor_scalar(
                out=ptokb[:, j, :], in0=ptokb[:, j, :], scalar1=-1.0,
                scalar2=float(CAP - 1), op0=OP.add, op1=OP.min),
                reads=["ixs"], writes=["ixs"])
        for j in range(KC):
            p.do("vector", lambda e, j=j, hh=hh: e.tensor_tensor(
                out=growb[:, j, :], in0=ptokb[:, j, :], in1=erows[:, hh, :],
                op=OP.add), reads=["ixs", "erows"], writes=["ixs"])
        for j in range(KC):
            tt = hb + j
            p.do("vector", lambda e, j=j, tt=tt: e.tensor_scalar(
                out=selb[:, j, :], in0=wmsk[:, tt, :], scalar1=0.0,
                scalar2=None, op0=OP.is_gt),
                reads=["wm%d" % tt], writes=["ixs"])
        for j in range(KC):
            p.do("vector", lambda e, j=j: e.tensor_scalar(
                out=scrb[:, j, :], in0=growb[:, j, :], scalar1=-1.0,
                scalar2=BIGC, op0=OP.mult, op1=OP.add),
                reads=["ixs"], writes=["ixs"])
        for j in range(KC):
            p.do("vector", lambda e, j=j: e.tensor_tensor(
                out=scrb[:, j, :], in0=scrb[:, j, :], in1=selb[:, j, :],
                op=OP.mult), reads=["ixs"], writes=["ixs"])
        for j in range(KC):
            p.do("vector", lambda e, j=j: e.tensor_reduce(
                out=colsb[:, j, 0:1], in_=scrb[:, j, :], axis=AX.X,
                op=OP.max), reads=["ixs"], writes=["ixs"])
        for j in range(KC):
            p.do("vector", lambda e, j=j: e.tensor_scalar(
                out=scrb[:, j, :], in0=growb[:, j, :], scalar1=1.0,
                scalar2=None, op0=OP.add), reads=["ixs"], writes=["ixs"])
        for j in range(KC):
            p.do("vector", lambda e, j=j: e.tensor_tensor(
                out=scrb[:, j, :], in0=scrb[:, j, :], in1=selb[:, j, :],
                op=OP.mult), reads=["ixs"], writes=["ixs"])
        for j in range(KC):
            p.do("vector", lambda e, j=j: e.tensor_reduce(
                out=colsb[:, j, 1:2], in_=scrb[:, j, :], axis=AX.X,
                op=OP.max), reads=["ixs"], writes=["ixs"])
        for j in range(KC):
            p.do("vector", lambda e, j=j: e.tensor_scalar(
                out=colsb[:, j, 2:3], in0=colsb[:, j, 0:1], scalar1=-1.0,
                scalar2=BIGC, op0=OP.mult, op1=OP.add),
                reads=["ixs"], writes=["ixs"])
        for j in range(KC):
            p.do("vector", lambda e, j=j: e.tensor_scalar(
                out=colsb[:, j, 3:4], in0=colsb[:, j, 1:2], scalar1=-1.0,
                scalar2=None, op0=OP.add), reads=["ixs"], writes=["ixs"])
        for j in range(KC):
            tt = hb + j
            p.do("vector", lambda e, j=j, tt=tt: e.tensor_copy(
                out=cidx[:, tt, :], in_=colsb[:, j, 2:4]),
                reads=["ixs"], writes=["cidx%d" % tt])
        for j in range(KC):
            p.do("vector", lambda e, j=j: e.tensor_scalar(
                out=scrb[:, j, :], in0=growb[:, j, :],
                scalar1=colsb[:, j, 2:3], scalar2=None, op0=OP.is_equal),
                reads=["ixs"], writes=["ixs"])
        for j in range(KC):
            tt = hb + j
            p.do("vector", lambda e, j=j, tt=tt: e.tensor_tensor(
                out=scrb[:, j, :], in0=scrb[:, j, :], in1=wmsk[:, tt, :],
                op=OP.mult), reads=["ixs", "wm%d" % tt], writes=["ixs"])
        for j in range(KC):
            tt = hb + j
            p.do("vector", lambda e, j=j, tt=tt: e.tensor_reduce(
                out=wab[:, tt, 0:1], in_=scrb[:, j, :], axis=AX.X,
                op=OP.add), reads=["ixs"], writes=["wab%d" % tt])
        for j in range(KC):
            p.do("vector", lambda e, j=j: e.tensor_scalar(
                out=scrb[:, j, :], in0=growb[:, j, :],
                scalar1=colsb[:, j, 3:4], scalar2=None, op0=OP.is_equal),
                reads=["ixs"], writes=["ixs"])
        for j in range(KC):
            tt = hb + j
            p.do("vector", lambda e, j=j, tt=tt: e.tensor_tensor(
                out=scrb[:, j, :], in0=scrb[:, j, :], in1=wmsk[:, tt, :],
                op=OP.mult), reads=["ixs", "wm%d" % tt], writes=["ixs"])
        for j in range(KC):
            tt = hb + j
            p.do("vector", lambda e, j=j, tt=tt: e.tensor_reduce(
                out=wab[:, tt, 1:2], in_=scrb[:, j, :], axis=AX.X,
                op=OP.add), reads=["ixs"], writes=["wab%d" % tt])
        for j in range(KC):
            tt = hb + j
            for k in range(2):
                p.do("gpsimd", lambda e, tt=tt, k=k:
                     e.indirect_dma_start(
                         out=d_tok, out_offset=IOA(cidx[:, tt, k:k + 1], 0),
                         in_=toki[:, tt:tt + 1], in_offset=None),
                     reads=["cidx%d" % tt, "toki", "dtok%d" % hh],
                     dma=True, defer=True)
        for k2 in range(2):
            p.pending_dma.setdefault("gpsimd", []).append(("w", "dtok%d" % hh))
        p.dma_flush("gpsimd")

    def emit_experts(hh):
        base = hh * 6400

        def rbk(ei):
            par = ei % 2
            p.do("gpsimd", lambda e, ei=ei, par=par: e.dma_start(
                out=gi32[:, par, 0:1],
                in_=d_tok[base + ei * CAP:base + ei * CAP + 128, :]),
                reads=["dtok%d" % hh], writes=["gi%d" % par], dma=True,
                defer=True)
            p.do("gpsimd", lambda e, ei=ei, par=par: e.dma_start(
                out=gi32[0:64, par, 1:2],
                in_=d_tok[base + ei * CAP + 128:base + ei * CAP + 192, :]),
                reads=["dtok%d" % hh], writes=["gi%d" % par], dma=True,
                defer=True)

        def gth(ei):
            par = ei % 2
            p.do("gpsimd", lambda e, par=par: e.indirect_dma_start(
                out=gbA[:, par, :], out_offset=None, in_=d_h2s,
                in_offset=IOA(gi32[:, par, 0:1], 0)),
                reads=["gi%d" % par, "h2s"], writes=["gbA%d" % par],
                dma=True, defer=True)
            p.do("gpsimd", lambda e, par=par: e.indirect_dma_start(
                out=gbB[:, par, :], out_offset=None, in_=d_h2s,
                in_offset=IOA(gi32[0:64, par, 1:2], 0)),
                reads=["gi%d" % par, "h2s"], writes=["gbB%d" % par],
                dma=True, defer=True)
        TRS = [(pC[:, 0:CAP], "ps2_0"), (pC[:, 512:512 + CAP], "ps2_1"),
               (pD[:, 0:CAP], "ps3_0"), (pD[:, 512:512 + CAP], "ps3_1")]

        def trans(ei):
            par = ei % 2
            for c in range(KC):
                slot, bn = TRS[c % 4]
                slotr = slot.bitcast(F32R)
                p.do("tensor", lambda e, c=c, par=par, slotr=slotr: e.transpose(
                    slotr[:, 0:128], gbA[:, par, c * 128:(c + 1) * 128],
                    identr[:]),
                    reads=["gbA%d" % par, "identr"], writes=[bn])
                p.do("tensor", lambda e, c=c, par=par, slotr=slotr: e.transpose(
                    slotr[:, 128:192], gbB[0:64, par, c * 128:(c + 1) * 128],
                    identr[0:64, 0:64]),
                    reads=["gbB%d" % par, "identr"], writes=[bn])
                p.do("vector", lambda e, c=c, par=par, slot=slot: e.tensor_copy(
                    out=lhsT[:, par, c, :], in_=slot),
                    reads=[bn], writes=["lhs%d" % par])
        rbk(0)
        p.dma_flush("gpsimd")
        gth(0)
        rbk(1)
        p.dma_flush("gpsimd")
        gth(1)
        rbk(2)
        p.dma_flush("gpsimd")
        trans(0)
        for ei in range(NE):
            par = ei % 2
            qeng = "sync" if par == 0 else "scalar"
            wres = "web%d" % par
            if ei + 2 < NE:
                gth(ei + 2)
            if ei + 3 < NE:
                rbk(ei + 3)
            p.dma_flush("gpsimd")
            if ei + 1 < NE:
                trans(ei + 1)
            pblk = pAB[0]
            bka, bkb = bankres(0)
            b1a, b1b = bankres(1)

            def fe_mms(e, par=par, pblk=pblk):
                for c in range(KC):
                    for nn in range(2):
                        e.matmul(pblk[:, nn * 512:(nn + 1) * 512],
                                 lhsT[:, par, c, 0:128],
                                 web2[:, par, c, nn * 512:(nn + 1) * 512],
                                 start=(c == 0), stop=(c == KC - 1))
                for c in range(KC):
                    for nn in range(2):
                        mm = e.matmul(pB[0:64, nn * 512:(nn + 1) * 512],
                                      lhsT[:, par, c, 128:192],
                                      web2[:, par, c, nn * 512:(nn + 1) * 512],
                                      start=(c == 0), stop=(c == KC - 1))
                return mm
            p.do("tensor", fe_mms, reads=["lhs%d" % par, wres],
                 writes=[bka, bkb, b1a, b1b])
            wei = ei + 2
            if wei < NE or hh == 0:
                wsrc = wei if wei < NE else wei - NE
                p.do(qeng, lambda e, wsrc=wsrc, par=par: e.dma_start(
                    out=web2[:, par],
                    in_=d_werb[wsrc:wsrc + 1].rearrange(
                        "o (c q) n -> q (o c) n", q=128)),
                    reads=["werb%d" % wsrc], writes=[wres], dma=True,
                    defer=True)
            ctres = "ctr%s%d" % ("E" if par == 0 else "O", hh)
            p.do("vector", lambda e, par=par, pblk=pblk: e.tensor_copy(
                out=ctrb[:, par, :], in_=pblk[:]),
                reads=[bka, bkb], writes=["ctrb%d" % par])
            p.do("vector", lambda e, par=par: e.tensor_copy(
                out=ctrb1[:, par, :], in_=pB[0:64, :]),
                reads=[b1a, b1b], writes=["ctrb1%d" % par])
            p.do(qeng, lambda e, ei=ei, par=par: e.dma_start(
                out=d_ctr[base + ei * CAP:base + ei * CAP + 128, :],
                in_=ctrb[:, par, :]),
                reads=["ctrb%d" % par], writes=[ctres], dma=True, defer=True)
            p.do(qeng, lambda e, ei=ei, par=par: e.dma_start(
                out=d_ctr[base + ei * CAP + 128:base + ei * CAP + 192, :],
                in_=ctrb1[:, par, :]),
                reads=["ctrb1%d" % par], writes=[ctres], dma=True, defer=True)
            p.dma_flush(qeng)

    def emit_combine(hh):
        for i in range(8):
            tt = hh * KC + i
            tp = i % 2
            xres = "xtile%d" % tp
            gres = "gc%d" % tp
            for k in range(2):
                p.do("gpsimd", lambda e, tt=tt, tp=tp, k=k:
                     e.indirect_dma_start(
                         out=gct[:, tp, k, :], out_offset=None, in_=d_ctr,
                         in_offset=IOA(cidx[:, tt, k:k + 1], 0)),
                     reads=["cidx%d" % tt, "ctrE%d" % hh, "ctrO%d" % hh],
                     writes=[gres], dma=True, defer=True)
            p.dma_flush("gpsimd")
            p.do("sync", lambda e, tt=tt, tp=tp: e.dma_start(
                out=xtile[:, tp, :], in_=d_x2s[tt * 128:(tt + 1) * 128, :]),
                reads=["x2s"], writes=[xres], dma=True)
            p.do("vector", lambda e, tt=tt, tp=tp: e.tensor_scalar(
                out=escr[:, tp, :], in0=gct[:, tp, 0, :],
                scalar1=wab[:, tt, 0:1], scalar2=None, op0=OP.mult),
                reads=[gres, "wab%d" % tt], writes=["escr%d" % tp])
            p.do("vector", lambda e, tt=tt, tp=tp: e.scalar_tensor_tensor(
                out=escr[:, tp, :], in0=gct[:, tp, 1, :],
                scalar=wab[:, tt, 1:2], in1=escr[:, tp, :],
                op0=OP.mult, op1=OP.add),
                reads=[gres, "wab%d" % tt, "escr%d" % tp], writes=["escr%d" % tp])
            p.do("vector", lambda e, tt=tt, tp=tp: e.scalar_tensor_tensor(
                out=escr[:, tp, :], in0=escr[:, tp, :],
                scalar=recipc[:, tt:tt + 1], in1=modsl(5),
                op0=OP.mult, op1=OP.mult),
                reads=["escr%d" % tp, "recip", "modB"], writes=["escr%d" % tp])
            p.do("vector", lambda e, tp=tp: e.tensor_tensor(
                out=xtile[:, tp, :], in0=xtile[:, tp, :], in1=escr[:, tp, :],
                op=OP.add), reads=[xres, "escr%d" % tp], writes=[xres])
            p.do("sync", lambda e, tt=tt, tp=tp: e.dma_start(
                out=d_out[tt * 128:(tt + 1) * 128, :], in_=xtile[:, tp, :]),
                reads=[xres], writes=["outd"], dma=True)

    emit_ln_router(0)
    emit_index(0)
    if stage < 8:
        stB.close()
        return _emit(nc, p, None)
    emit_ln_router(1)
    emit_index(1)
    emit_experts(0)
    emit_combine(0)
    emit_experts(1)
    emit_combine(1)

    stB.close()
    return _emit(nc, p, p.w["outd"])


def _emit(nc, p, final_out):
    with (
        nc.semaphore("s_sync") as s_sync,
        nc.semaphore("s_gp") as s_gp,
        nc.semaphore("s_sdl") as s_sdl,
        nc.semaphore("s_gdl") as s_gdl,
        nc.semaphore("s_scd") as s_scd,
        nc.semaphore("s_ved") as s_ved,
        nc.semaphore("s_v") as s_v,
        nc.semaphore("s_s") as s_s,
        nc.semaphore("s_t") as s_t,
        nc.Block() as block,
    ):
        sems = {"sync": s_sync, "gpsimd": s_gp, "syncdl": s_sdl, "gpdl": s_gdl,
                "scdl": s_scd, "vedl": s_ved, "vector": s_v, "scalar": s_s,
                "tensor": s_t}

        def runner(name, final_waits=()):
            def _run(eng):
                for ws, fn, inc, sem in p.ops[name]:
                    for pe, val in ws:
                        eng.wait_ge(sems[pe], val)
                    if fn is None:
                        eng.sem_inc(sems[sem], inc)
                    else:
                        inst = fn(eng)
                        inst.then_inc(sems[sem], inc)
                for pe, val in final_waits:
                    if pe != name:
                        eng.wait_ge(sems[pe], val)
            return _run

        fw = ([final_out] if final_out else []) + [(e, p.cnt[e]) for e in SEMS]
        block.sync(runner("sync", final_waits=fw))
        block.gpsimd(runner("gpsimd"))
        block.vector(runner("vector"))
        block.scalar(runner("scalar"))
        block.tensor(runner("tensor"))

    return nc


def stage_inputs(inputs):
    f = np.ascontiguousarray
    x = inputs["L0_attn_x"]
    kx = inputs["L0_attn_conv_x_k"][:, 0, :].T.reshape(KC, 128, 4).transpose(1, 0, 2).reshape(128, 32)
    kz = inputs["L0_attn_conv_z_k"][:, 0, :].T.reshape(KC, 128, 4).transpose(1, 0, 2).reshape(128, 32)
    alog = inputs["L0_attn_A_log"].reshape(KC, 128, NST).transpose(1, 0, 2).reshape(128, 128)
    bdt = inputs["L0_attn_b_dt"].reshape(KC, 128).T
    dd = inputs["L0_attn_D"].reshape(KC, 128).T
    rwT = inputs["L0_ffn_router_w"].T
    common = dict(
        w_in=f(inputs["L0_attn_W_in"]),
        kx=f(kx), kz=f(kz),
        w_xproj=f(inputs["L0_attn_W_xproj"]),
        w_dt=f(inputs["L0_attn_W_dt"]),
        b_dt=f(bdt), a_log=f(alog), d_d=f(dd),
        w_out=f(inputs["L0_attn_W_out"]),
        b_out=f(inputs["L0_attn_b_out"].reshape(1, D)),
        r_wT=f(rwT),
        w_e=f(inputs["L0_ffn_expert_weights"]),
        wmod=f(inputs["L0_wmod"]),
    )
    maps = []
    for b in range(8):
        m = dict(common)
        m["x"] = f(np.asarray(x[b]))
        m["c_row"] = f(np.asarray(inputs["L0_c"][b]).reshape(KC, 128).T)
        maps.append(m)
    return maps


def kernel(**inputs):
    maps = stage_inputs(inputs)
    nc = build_program(debug=False)
    res = run_bass_kernel_spmd(nc, maps, core_ids=list(range(8)))
    return np.stack([np.asarray(r["out"]) for r in res.results], axis=0)
